# version 01
# baseline (speedup 1.0000x reference)
import math
import contextlib
import numpy as np
import ml_dtypes
import concourse.bass as bass
import concourse.mybir as mybir
from concourse.bass_utils import run_bass_kernel_spmd

F32 = mybir.dt.float32
BF16 = mybir.dt.bfloat16
I32 = mybir.dt.int32
AF = mybir.ActivationFunctionType
AL = mybir.AluOpType
AX = mybir.AxisListType.X

D = 1024
T = 2048
NS = 8
LN_EPS = 1e-5
COS_EPS = 1e-6
PI = math.pi

ENGS = ("tensor", "vector", "scalar", "gpsimd", "sync")
CHANNELS = ("ld", "lg", "w0", "w1", "u0", "u1", "out")


class Prog:

    def __init__(self, nc):
        self.nc = nc
        self.ops = {e: [] for e in ENGS}
        self.cnt = {e: 0 for e in ENGS}
        self.ccnt = {c: 0 for c in CHANNELS}
        self.sems = {}

    def emit(self, eng, fn, inc=0):
        self.ops[eng].append((fn, inc, eng in ("vector", "scalar")))
        if inc:
            self.cnt[eng] += inc
        return self.cnt[eng]

    def wait(self, eng, prod, tok):
        if tok <= 0 or prod == eng:
            return
        self.ops[eng].append(
            (lambda e, prod=prod, tok=tok: e.wait_ge(self.sems[prod], tok), 0, False))

    def waitc(self, eng, ch, tok):
        if tok <= 0:
            return
        self.ops[eng].append(
            (lambda e, ch=ch, tok=tok: e.wait_ge(self.sems[ch], tok), 0, False))

    def tok(self, eng):
        return self.cnt[eng]

    def barrier(self, engs=("tensor", "vector", "scalar", "gpsimd", "sync")):
        toks = {e: self.cnt[e] for e in ENGS}
        ctoks = {c: self.ccnt[c] for c in CHANNELS}
        for e in engs:
            for prod in ENGS:
                if prod != e and toks[prod] > 0:
                    self.wait(e, prod, toks[prod])
            for c in CHANNELS:
                if ctoks[c] > 0:
                    self.waitc(e, c, ctoks[c])

    def finalize(self):
        nc = self.nc
        with contextlib.ExitStack() as st:
            for e in ENGS:
                self.sems[e] = st.enter_context(nc.semaphore(f"sem_{e}"))
            for c in CHANNELS:
                self.sems[c] = st.enter_context(nc.semaphore(f"ch_{c}"))
            with nc.Block() as block:
                for e in ENGS:
                    def run(proxy, e=e):
                        sem = self.sems[e]
                        for fn, inc, dr in self.ops[e]:
                            inst = fn(proxy)
                            if inc:
                                inst.then_inc(sem, inc)
                            if dr:
                                proxy.drain()
                    getattr(block, e)(run)


def build_core_program(phases=4):
    nc = bass.Bass("TRN2", target_bir_lowering=False)

    def din(name, shape, dtype=F32):
        return nc.dram_tensor(name, list(shape), dtype, kind="ExternalInput")

    x_in = din("x_in", (D, T))
    l0_wq16 = din("l0_wq16", (D, 16))
    l0_wk16 = din("l0_wk16", (D, 16))
    l0_wv = din("l0_wv", (D, D))
    l0_wo = din("l0_wo", (D, D))
    l0_ln1w = din("l0_ln1w", (128, 8)); l0_ln1b = din("l0_ln1b", (128, 8))
    l0_ln2w = din("l0_ln2w", (128, 8)); l0_ln2b = din("l0_ln2b", (128, 8))
    l0_W1 = din("l0_W1", (D, 4096)); l0_b1 = din("l0_b1", (128, 32))
    l0_W2 = din("l0_W2", (4096, D)); l0_b2 = din("l0_b2", (128, 8))
    f0_Wp1 = din("f0_Wp1", (D, 128)); f0_bp1 = din("f0_bp1", (128, 1))
    f0_Wp2 = din("f0_Wp2", (128, 8)); f0_bp2 = din("f0_bp2", (1, 8))
    f0_W1 = din("f0_W1", (8, D, 2048)); f0_b1 = din("f0_b1", (8, 128, 16))
    f0_W2 = din("f0_W2", (8, 2048, D)); f0_b2 = din("f0_b2", (8, 128, 8))
    f0_Wo = din("f0_Wo", (D, D)); f0_bo = din("f0_bo", (128, 8))
    l1_wqkv = din("l1_wqkv", (D, 3072))
    l1_wo = din("l1_wo", (D, D))
    f1_gate = din("f1_gate", (D, 8))
    f1_wg = din("f1_wg", (8, D, 4096))
    f1_wu = din("f1_wu", (8, D, 4096))
    f1_wd = din("f1_wd", (8, 4096, D))
    c_ones_bf = din("c_ones_bf", (128, 128), BF16)
    c_ones_f = din("c_ones_f", (128, 128))
    c_ident = din("c_ident", (128, 128))
    c_sel16 = din("c_sel16", (16, 1024))
    c_bcast8 = din("c_bcast8", (8, 1024))
    c_maskT = din("c_maskT", (128, 64))
    c_invtot = din("c_invtot", (128, T))
    c_idxbig = din("c_idxbig", (1, 8))
    c_flt = din("c_flt", (128, 2))
    out_d = nc.dram_tensor("out", [D, T], F32, kind="ExternalOutput")

    SB_BASE = 16512

    def sb(name, shape, dtype, off):
        assert off % 32 == 0, (name, off)
        return nc.alloc_sbuf_tensor_at(name, list(shape), dtype,
                                       offset=off + SB_BASE)

    SB_END = 212832
    x = sb("x", (128, NS * T), F32, 0)
    o = 65536
    def psb(name, shape, dtype, nbytes):
        nonlocal o
        t = sb(name, shape, dtype, o)
        o += nbytes
        return t
    ones_bf = psb("ones_bf", (128, 128), BF16, 256)
    ones_f = psb("ones_f", (128, 128), F32, 512)
    ident = psb("ident", (128, 128), F32, 512)
    ln1w = psb("ln1w", (128, 8), F32, 32)
    ln1b = psb("ln1b", (128, 8), F32, 32)
    ln2w = psb("ln2w", (128, 8), F32, 32)
    ln2b = psb("ln2b", (128, 8), F32, 32)
    b1sb = psb("b1sb", (128, 32), F32, 128)
    b2sb = psb("b2sb", (128, 8), F32, 32)
    wp1sb = psb("wp1sb", (128, 1024), F32, 4096)
    bp1sb = psb("bp1sb", (128, 1), F32, 32)
    wp2sb = psb("wp2sb", (128, 8), F32, 32)
    bp2sb = psb("bp2sb", (1, 8), F32, 32)
    f0b1s = psb("f0b1s", (128, 16), F32, 64)
    f0b2s = psb("f0b2s", (128, 8), F32, 32)
    f0bos = psb("f0bos", (128, 8), F32, 32)
    f0bo3 = psb("f0bo3", (128, 8), F32, 32)
    sel16 = psb("sel16", (16, 1024), F32, 4096)
    bcast8 = psb("bcast8", (8, 1024), F32, 4096)
    maskT = psb("maskT", (128, 64), F32, 256)
    idxbig = psb("idxbig", (1, 8), F32, 32)
    gatebf = psb("gatebf", (128, 64), BF16, 128)
    comb = psb("comb", (8, T), F32, 8192)
    carry_ck = psb("carry_ck", (128, 8), F32, 32)
    carry_sk = psb("carry_sk", (128, 8), F32, 32)
    carry16 = psb("carry16", (16, 4), F32, 32)
    gsc = psb("gsc", (128, 64), F32, 256)
    gt = psb("gt", (1, 32), F32, 128)
    seli = psb("seli", (1, 2), I32, 32)
    cflt = psb("cflt", (128, 2), F32, 32)
    nc.const_aps.register(F32, float(LN_EPS), cflt[:, 0:1])
    nc.const_aps.register(F32, float(PI / 2), cflt[:, 1:2])
    assert o <= 89504, o
    S0 = 89504

    pb = [nc.alloc_psum_tensor(f"pb{i}", [128, 512], F32) for i in range(8)]

    p = Prog(nc)

    bank_w = [0] * 8
    bank_r = [[] for _ in range(8)]

    def bank_prep(i):
        for (eng, tok) in bank_r[i]:
            p.wait("tensor", eng, tok)
        bank_r[i] = []

    def bank_done(i):
        bank_w[i] = p.tok("tensor")

    def bank_read(i, eng):
        p.wait(eng, "tensor", bank_w[i])

    def bank_rdone(i, eng):
        bank_r[i].append((eng, p.tok(eng)))

    def dma(queue, ch, out_ap, in_ap):
        def f(e, a=out_ap, b=in_ap, ch=ch):
            return e.dma_start(out=a, in_=b).then_inc(p.sems[ch], 16)
        p.emit(queue, f, inc=0)
        p.ccnt[ch] += 16
        return p.ccnt[ch]

    def mm(lhsT, rhs, out, start, stop, inc=0):
        return p.emit("tensor",
                      lambda e, a=lhsT, b=rhs, c=out, s1=start, s2=stop:
                      e.matmul(c, a, b, start=s1, stop=s2), inc=inc)

    def ve(fn, inc=1):
        return p.emit("vector", fn, inc=inc)

    def se(fn, inc=1):
        return p.emit("scalar", fn, inc=inc)

    def mm_group(bank_i, lhsT_rhs_list, out):
        bank_prep(bank_i)
        n = len(lhsT_rhs_list)
        for i, (a, b) in enumerate(lhsT_rhs_list):
            mm(a, b, out, start=(i == 0), stop=(i == n - 1),
               inc=(1 if i == n - 1 else 0))
        bank_done(bank_i)

    def layernorm_cg(Wc, src, xc, outs, xbf2, sq2, rsc, tmpln):
        mm_tok = [0] * NS
        bank_prep(0)
        for s in range(NS):
            if s >= 2:
                p.wait("vector", "tensor", mm_tok[s - 2])
            buf = xbf2[:, (s % 2) * Wc:(s % 2) * Wc + Wc]
            tv = ve(lambda e, a=buf, b=src(s): e.tensor_copy(a, b))
            p.wait("tensor", "vector", tv)
            mm_tok[s] = mm(ones_bf[:, :], buf, pb[0][:, :Wc],
                           start=(s == 0), stop=(s == NS - 1), inc=1)
        bank_done(0)
        bank_read(0, "vector")
        for s in range(NS):
            ve(lambda e, a=xc(s), b=src(s):
               e.scalar_tensor_tensor(a, pb[0][:, :Wc], -1.0 / 1024.0, b,
                                      AL.mult, AL.add))
        xc_tok = p.tok("vector")
        bank_rdone(0, "vector")
        mm2 = [0] * NS
        bank_prep(1)
        p.wait("scalar", "vector", xc_tok)
        for s in range(NS):
            if s >= 2:
                p.wait("scalar", "tensor", mm2[s - 2])
            buf = sq2[:, (s % 2) * Wc:(s % 2) * Wc + Wc]
            ts = se(lambda e, a=buf, b=xc(s): e.activation(a, b, AF.Square))
            p.wait("tensor", "scalar", ts)
            mm2[s] = mm(ones_bf[:, :], buf, pb[1][:, :Wc],
                        start=(s == 0), stop=(s == NS - 1), inc=1)
        bank_done(1)
        bank_read(1, "scalar")
        tsq = se(lambda e: e.activation(rsc[:, :Wc], pb[1][:, :Wc], AF.Sqrt,
                                        bias=float(LN_EPS), scale=1.0 / 1024.0))
        bank_rdone(1, "scalar")
        p.wait("vector", "scalar", tsq)
        ve(lambda e: e.reciprocal(rsc[:, :Wc], rsc[:, :Wc]))
        for s in range(NS):
            for (out_ap, wcol, bcol) in outs:
                if wcol is None:
                    ve(lambda e, a=out_ap(s), b=xc(s):
                       e.tensor_mul(a, b, rsc[:, :Wc]))
                else:
                    ve(lambda e, b=xc(s): e.tensor_mul(tmpln[:, :Wc], b, rsc[:, :Wc]))
                    ve(lambda e, a=out_ap(s), w=wcol(s), bb=bcol(s):
                       e.tensor_scalar(a, tmpln[:, :Wc], w, bb, AL.mult, AL.add))
        return p.tok("vector")

    def xap(s, g, W):
        return x[:, s * T + g * W: s * T + g * W + W]

    dma("sync", "ld", x[:, :], bass.AP(x_in, 0, [[T, 128], [128 * T, NS], [1, T]]))
    for (sbuf, dram) in [(ones_bf, c_ones_bf), (ones_f, c_ones_f),
                         (ident, c_ident), (ln1w, l0_ln1w), (ln1b, l0_ln1b),
                         (ln2w, l0_ln2w), (ln2b, l0_ln2b), (b1sb, l0_b1),
                         (b2sb, l0_b2), (bp1sb, f0_bp1), (wp2sb, f0_Wp2),
                         (bp2sb, f0_bp2), (f0bos, f0_bo), (sel16, c_sel16),
                         (bcast8, c_bcast8), (maskT, c_maskT),
                         (idxbig, c_idxbig), (cflt, c_flt)]:
        dma("sync", "ld", sbuf[:, :], dram.ap())
    dma("sync", "ld", wp1sb[:, :],
        bass.AP(f0_Wp1, 0, [[128, 128], [128 * 128, 8], [1, 128]]))
    dma("gpsimd", "lg", gatebf[:, :],
        bass.AP(f1_gate, 0, [[8, 128], [8 * 128, 8], [1, 8]]))

    if phases >= 1:
        G1 = 512
        a_y0 = sb("a_y0", (128, 8 * G1), F32, S0)
        a_y1 = sb("a_y1", (128, 8 * G1), BF16, S0 + 16384)
        a_wkrep = sb("a_wkrep", (128, 8192), BF16, S0 + 24576)
        a_wv = sb("a_wv", (128, 8192), BF16, S0 + 40960)
        a_h1 = sb("a_h1", (128, 16384), BF16, S0 + 24576)
        a_wo = sb("a_wo", (128, 8192), BF16, S0 + 57344)
        a_ckv = sb("a_ckv", (128, 8 * G1), F32, S0 + 73728)
        a_xc2 = sb("a_xc2", (128, 8 * G1), F32, S0 + 73728)
        a_skv = sb("a_skv", (128, 8 * G1), F32, S0 + 90112)
        a_w1c = sb("a_w1c", (128, 2048), BF16, S0 + 90112)
        a_w2c = sb("a_w2c", (128, 2048), BF16, S0 + 94208)
        a_heads = sb("a_heads", (128, 8 * G1), BF16, S0 + 16384)
        a_q16 = sb("a_q16", (16, 2560), F32, S0 + 106496)
        a_xbf2 = sb("a_xbf2", (128, 1024), BF16, S0 + 106496)
        a_sq2 = sb("a_sq2", (128, 1024), BF16, S0 + 108544)
        a_tk = sb("a_tk", (128, 512), F32, S0 + 116736)
        a_tmpln = sb("a_tmpln", (128, 512), F32, S0 + 116736)
        a_tmp1 = sb("a_tmp1", (128, 512), F32, S0 + 118784)
        a_rsc = sb("a_rsc", (128, 512), F32, S0 + 118784)
        a_wq16 = sb("a_wq16", (128, 256), BF16, S0 + 120832)
        assert S0 + 121344 <= SB_END

        dma("gpsimd", "lg", a_wq16[:, 0:128],
            bass.AP(l0_wq16, 0, [[16, 128], [16 * 128, 8], [1, 16]]))
        dma("gpsimd", "lg", a_wq16[:, 128:256],
            bass.AP(l0_wk16, 0, [[16, 128], [16 * 128, 8], [1, 16]]))
        dma("gpsimd", "lg", a_wo[:, :],
            bass.AP(l0_wo, 0, [[1024, 128], [128 * 1024, 8], [1, 1024]]))
        dma("gpsimd", "lg", a_wv[:, :],
            bass.AP(l0_wv, 0, [[1024, 128], [128 * 1024, 8], [1, 1024]]))
        t_ld0 = p.ccnt["ld"]
        t_lg0 = p.ccnt["lg"]
        for e in ("vector", "scalar", "tensor", "gpsimd"):
            p.waitc(e, "ld", t_ld0)
            p.waitc(e, "lg", t_lg0)
        ve(lambda e: e.tensor_scalar_mul(f0bo3[:, :], f0bos[:, :], 0.3))
        ve(lambda e: e.memset(carry_ck[:, :], 0.0))
        ve(lambda e: e.memset(carry_sk[:, :], 0.0))
        ve(lambda e: e.memset(carry16[:, :], 0.0))

        def q16s(i):
            return a_q16[:, i * 512:(i + 1) * 512]

        for g in range(4):
            if g > 0:
                p.barrier(engs=("gpsimd",))
                dma("gpsimd", "lg", a_wv[:, :],
                    bass.AP(l0_wv, 0, [[1024, 128], [128 * 1024, 8], [1, 1024]]))
            twv = p.ccnt["lg"]
            p.waitc("vector", "lg", twv)
            for dc in range(NS):
                ve(lambda e, dc=dc: e.tensor_copy(
                    a_wkrep[:, dc * 1024:(dc + 1) * 1024],
                    bass.AP(a_wq16, 128 + dc * 16, [[256, 128], [1, 16], [0, 64]])))
            twk = p.tok("vector")
            p.wait("tensor", "vector", twk)
            p.waitc("tensor", "lg", twv)

            layernorm_cg(G1, lambda s, g=g: xap(s, g, G1),
                         lambda s: a_y0[:, s * G1:(s + 1) * G1],
                         [(lambda s: a_y0[:, s * G1:(s + 1) * G1], None, None)],
                         a_xbf2, a_sq2, a_rsc, a_tmpln)
            t_ln1 = layernorm_cg(
                G1, lambda s: a_y0[:, s * G1:(s + 1) * G1],
                lambda s: a_xc2[:, s * G1:(s + 1) * G1],
                [(lambda s: a_y1[:, s * G1:(s + 1) * G1],
                  lambda s: ln1w[:, s:s + 1], lambda s: ln1b[:, s:s + 1])],
                a_xbf2, a_sq2, a_rsc, a_tmpln)
            p.wait("tensor", "vector", t_ln1)

            mm_group(2, [(a_wq16[:, dc * 16:dc * 16 + 16],
                          a_y1[:, dc * G1:(dc + 1) * G1]) for dc in range(NS)],
                     pb[2][0:16, :G1])
            mm_group(3, [(a_wq16[:, 128 + dc * 16:128 + dc * 16 + 16],
                          a_y1[:, dc * G1:(dc + 1) * G1]) for dc in range(NS)],
                     pb[3][0:16, :G1])
            bank_read(2, "scalar"); bank_read(3, "scalar")
            se(lambda e: e.activation(q16s(0)[0:16, :], pb[2][0:16, :G1], AF.Tanh))
            se(lambda e: e.activation(q16s(1)[0:16, :], q16s(0)[0:16, :], AF.Sin,
                                      bias=PI / 2, scale=PI / 4))
            se(lambda e: e.activation(q16s(2)[0:16, :], q16s(0)[0:16, :], AF.Sin,
                                      scale=PI / 4))
            se(lambda e: e.activation(q16s(0)[0:16, :], pb[3][0:16, :G1], AF.Tanh))
            se(lambda e: e.activation(q16s(3)[0:16, :], q16s(0)[0:16, :], AF.Sin,
                                      bias=PI / 2, scale=PI / 4))
            t_k16 = se(lambda e: e.activation(q16s(4)[0:16, :], q16s(0)[0:16, :],
                                              AF.Sin, scale=PI / 4))
            bank_rdone(2, "scalar"); bank_rdone(3, "scalar")
            p.wait("vector", "scalar", t_k16)
            for (i, cc) in ((3, 0), (4, 1)):
                ve(lambda e, i=i, cc=cc: e.tensor_tensor_scan(
                    q16s(i)[0:16, :], q16s(i)[0:16, :], q16s(i)[0:16, :],
                    carry16[0:16, cc:cc + 1], AL.add, AL.bypass))
                ve(lambda e, i=i, cc=cc: e.tensor_copy(
                    carry16[0:16, cc:cc + 1], a_q16[0:16, i * 512 + 511:i * 512 + 512]))
            ve(lambda e: e.tensor_mul(q16s(3)[0:16, :], q16s(1)[0:16, :], q16s(3)[0:16, :]))
            ve(lambda e: e.tensor_mul(q16s(4)[0:16, :], q16s(2)[0:16, :], q16s(4)[0:16, :]))
            ve(lambda e: e.tensor_add(q16s(3)[0:16, :], q16s(3)[0:16, :], q16s(4)[0:16, :]))
            ve(lambda e: e.tensor_scalar_add(q16s(3)[0:16, :], q16s(3)[0:16, :], COS_EPS))
            ve(lambda e: e.reciprocal(q16s(3)[0:16, :], q16s(3)[0:16, :]))
            ve(lambda e: e.tensor_mul(q16s(1)[0:16, :], q16s(1)[0:16, :], q16s(3)[0:16, :]))
            t_qd = ve(lambda e: e.tensor_mul(q16s(2)[0:16, :], q16s(2)[0:16, :],
                                             q16s(3)[0:16, :]))

            for s in range(NS):
                bk = 4 + (s % 2)
                mm_group(bk, [(a_wkrep[:, dc * 1024 + s * 128:dc * 1024 + s * 128 + 128],
                               a_y1[:, dc * G1:(dc + 1) * G1]) for dc in range(NS)],
                         pb[bk][:, :G1])
                bv_ = 6 + (s % 2)
                mm_group(bv_, [(a_wv[:, dc * 1024 + s * 128:dc * 1024 + s * 128 + 128],
                                a_y1[:, dc * G1:(dc + 1) * G1]) for dc in range(NS)],
                         pb[bv_][:, :G1])
                bank_read(bk, "scalar")
                se(lambda e, bk=bk: e.activation(a_tk[:, :], pb[bk][:, :G1], AF.Tanh))
                se(lambda e, s=s: e.activation(a_ckv[:, s * G1:(s + 1) * G1],
                                               a_tk[:, :], AF.Sin, bias=PI / 2,
                                               scale=PI / 4))
                tt = se(lambda e, s=s: e.activation(a_skv[:, s * G1:(s + 1) * G1],
                                                    a_tk[:, :], AF.Sin, scale=PI / 4))
                bank_rdone(bk, "scalar")
                p.wait("vector", "scalar", tt)
                bank_read(bv_, "vector")
                ve(lambda e, s=s, bv_=bv_: e.tensor_mul(
                    a_ckv[:, s * G1:(s + 1) * G1], a_ckv[:, s * G1:(s + 1) * G1],
                    pb[bv_][:, :G1]))
                ve(lambda e, s=s, bv_=bv_: e.tensor_mul(
                    a_skv[:, s * G1:(s + 1) * G1], a_skv[:, s * G1:(s + 1) * G1],
                    pb[bv_][:, :G1]))
                bank_rdone(bv_, "vector")
                for (buf, car) in ((a_ckv, carry_ck), (a_skv, carry_sk)):
                    ve(lambda e, s=s, buf=buf, car=car: e.tensor_tensor_scan(
                        buf[:, s * G1:(s + 1) * G1], buf[:, s * G1:(s + 1) * G1],
                        buf[:, s * G1:(s + 1) * G1], car[:, s:s + 1], AL.add, AL.bypass))
                    ve(lambda e, s=s, buf=buf, car=car: e.tensor_copy(
                        car[:, s:s + 1], buf[:, s * G1 + G1 - 1:s * G1 + G1]))

            p.wait("tensor", "vector", t_qd)
            for s in range(NS):
                mm_group(2, [(sel16[:, s * 128:(s + 1) * 128], q16s(1)[0:16, :])],
                         pb[2][:, :G1])
                mm_group(3, [(sel16[:, s * 128:(s + 1) * 128], q16s(2)[0:16, :])],
                         pb[3][:, :G1])
                bank_read(2, "vector"); bank_read(3, "vector")
                ve(lambda e, s=s: e.tensor_mul(a_tmp1[:, :],
                                               a_ckv[:, s * G1:(s + 1) * G1],
                                               pb[2][:, :G1]))
                ve(lambda e, s=s: e.tensor_mul(a_tk[:, :],
                                               a_skv[:, s * G1:(s + 1) * G1],
                                               pb[3][:, :G1]))
                ve(lambda e, s=s: e.tensor_add(a_heads[:, s * G1:(s + 1) * G1],
                                               a_tmp1[:, :], a_tk[:, :]))
                bank_rdone(2, "vector"); bank_rdone(3, "vector")
            t_heads = p.tok("vector")
            p.wait("tensor", "vector", t_heads)

            for dc in range(NS):
                bk = dc % 2
                mm_group(bk, [(a_wo[:, kc * 1024 + dc * 128:kc * 1024 + dc * 128 + 128],
                               a_heads[:, kc * G1:(kc + 1) * G1]) for kc in range(NS)],
                         pb[bk][:, :G1])
                bank_read(bk, "vector")
                ve(lambda e, dc=dc, bk=bk: e.tensor_add(
                    a_y0[:, dc * G1:(dc + 1) * G1], a_y0[:, dc * G1:(dc + 1) * G1],
                    pb[bk][:, :G1]))
                ve(lambda e, dc=dc, g=g: e.tensor_add(
                    xap(dc, g, G1), xap(dc, g, G1), a_y0[:, dc * G1:(dc + 1) * G1]))
                bank_rdone(bk, "vector")

            t_ln2 = layernorm_cg(
                G1, lambda s: a_y0[:, s * G1:(s + 1) * G1],
                lambda s: a_y0[:, s * G1:(s + 1) * G1],
                [(lambda s: a_y1[:, s * G1:(s + 1) * G1],
                  lambda s: ln2w[:, s:s + 1], lambda s: ln2b[:, s:s + 1])],
                a_xbf2, a_sq2, a_rsc, a_tmpln)
            p.wait("tensor", "vector", t_ln2)
            p.barrier(engs=("gpsimd",))

            t_w1use = [0] * 32
            for fc in range(32):
                half = fc % 2
                if fc >= 2:
                    p.wait("gpsimd", "tensor", t_w1use[fc - 2])
                tld = dma("gpsimd", "w%d" % half, a_w1c[:, half * 1024:half * 1024 + 1024],
                          bass.AP(l0_W1, fc * 128, [[4096, 128], [128 * 4096, 8], [1, 128]]))
                p.waitc("tensor", "w%d" % half, tld)
                bk = 2 + half
                mm_group(bk, [(a_w1c[:, half * 1024 + dc * 128:half * 1024 + dc * 128 + 128],
                               a_y1[:, dc * G1:(dc + 1) * G1]) for dc in range(NS)],
                         pb[bk][:, :G1])
                t_w1use[fc] = p.tok("tensor")
                bank_read(bk, "scalar")
                se(lambda e, fc=fc, bk=bk: e.activation(
                    a_h1[:, fc * G1:(fc + 1) * G1], pb[bk][:, :G1],
                    AF.Gelu_apprx_tanh, bias=b1sb[:, fc:fc + 1]))
                bank_rdone(bk, "scalar")
            t_h1 = p.tok("scalar")
            p.wait("tensor", "scalar", t_h1)

            for i in range(NS):
                bank_prep(i)
            t_w2use = [0] * 32
            for fc in range(32):
                half = fc % 2
                if fc >= 2:
                    p.wait("gpsimd", "tensor", t_w2use[fc - 2])
                tld = dma("gpsimd", "u%d" % half, a_w2c[:, half * 1024:half * 1024 + 1024],
                          bass.AP(l0_W2, fc * 128 * 1024, [[1024, 128], [1, 1024]]))
                p.waitc("tensor", "u%d" % half, tld)
                for dc in range(NS):
                    mm(a_w2c[:, half * 1024 + dc * 128:half * 1024 + dc * 128 + 128],
                       a_h1[:, fc * G1:(fc + 1) * G1], pb[dc][:, :G1],
                       start=(fc == 0), stop=(fc == 31),
                       inc=(1 if dc == NS - 1 else 0))
                t_w2use[fc] = p.tok("tensor")
            for dc in range(NS):
                bank_done(dc)
            t_down = p.tok("tensor")
            p.wait("vector", "tensor", t_down)
            for dc in range(NS):
                ve(lambda e, dc=dc, g=g: e.scalar_tensor_tensor(
                    xap(dc, g, G1), pb[dc][:, :G1], b2sb[:, dc:dc + 1],
                    xap(dc, g, G1), AL.add, AL.add))
                bank_rdone(dc, "vector")
            p.barrier()

    if phases >= 2:
        G2 = 256
        b_w1 = sb("b_w1", (128, 16384), BF16, S0)
        b_w2 = sb("b_w2", (128, 16384), BF16, S0 + 32768)
        b_wo = sb("b_wo", (128, 8192), BF16, S0 + 65536)
        b_yg = sb("b_yg", (128, 8 * G2), F32, S0 + 81920)
        b_ybf = sb("b_ybf", (128, 8 * G2), BF16, S0 + 90112)
        b_h1 = sb("b_h1", (128, 16 * G2), BF16, S0 + 94208)
        b_rsn = sb("b_rsn", (128, 8 * G2), BF16, S0 + 102400)
        b_xbf2 = sb("b_xbf2", (128, 512), BF16, S0 + 106496)
        b_sq2 = sb("b_sq2", (128, 512), BF16, S0 + 107520)
        b_rsc = sb("b_rsc", (128, 256), F32, S0 + 108544)
        b_tmpln = sb("b_tmpln", (128, 256), F32, S0 + 109568)
        assert S0 + 110592 <= SB_END

        xs = x[:, :].rearrange("p (s t) -> p s t", t=T)
        xl = xs[:, :, 2047:2048].rearrange("p s o -> p (s o)")
        tvl = ve(lambda e: e.tensor_copy(gsc[:, 0:8], xl))
        p.wait("tensor", "vector", tvl)
        mm_group(0, [(ones_f[:, :], gsc[:, 0:8])], pb[0][:, 0:8])
        bank_read(0, "vector")
        ve(lambda e: e.reduce_sum(gsc[:, 24:25], pb[0][:, 0:8], axis=AX))
        bank_rdone(0, "vector")
        ve(lambda e: e.tensor_scalar_mul(gsc[:, 24:25], gsc[:, 24:25], 1.0 / 1024.0))
        tvc = ve(lambda e: e.tensor_single_scalar(gsc[:, 8:16], gsc[:, 0:8],
                                                  gsc[:, 24:25], AL.subtract))
        p.wait("scalar", "vector", tvc)
        tsc = se(lambda e: e.activation(gsc[:, 16:24], gsc[:, 8:16], AF.Square))
        p.wait("tensor", "scalar", tsc)
        mm_group(1, [(ones_f[:, :], gsc[:, 16:24])], pb[1][:, 0:8])
        bank_read(1, "vector")
        ve(lambda e: e.reduce_sum(gsc[:, 25:26], pb[1][:, 0:8], axis=AX))
        bank_rdone(1, "vector")
        tvv = p.tok("vector")
        p.wait("scalar", "vector", tvv)
        tss = se(lambda e: e.activation(gsc[:, 26:27], gsc[:, 25:26], AF.Sqrt,
                                        bias=float(LN_EPS), scale=1.0 / 1024.0))
        p.wait("vector", "scalar", tss)
        ve(lambda e: e.reciprocal(gsc[:, 26:27], gsc[:, 26:27]))
        tyl = ve(lambda e: e.tensor_single_scalar(gsc[:, 32:40], gsc[:, 8:16],
                                                  gsc[:, 26:27], AL.mult))
        p.wait("tensor", "vector", tyl)
        mm_group(2, [(wp1sb[:, dc * 128:(dc + 1) * 128], gsc[:, 32 + dc:33 + dc])
                     for dc in range(NS)], pb[2][:, 0:1])
        bank_read(2, "scalar")
        thp = se(lambda e: e.activation(gsc[:, 27:28], pb[2][:, 0:1], AF.Gelu,
                                        bias=bp1sb[:, 0:1]))
        bank_rdone(2, "scalar")
        p.wait("tensor", "scalar", thp)
        mm_group(3, [(gsc[:, 27:28], wp2sb[:, :])], pb[3][0:1, 0:8])
        bank_read(3, "vector")
        ve(lambda e: e.tensor_add(gt[0:1, 0:8], pb[3][0:1, 0:8], bp2sb[0:1, :]))
        bank_rdone(3, "vector")
        ve(lambda e: e.reduce_max(gt[0:1, 8:9], gt[0:1, 0:8], axis=AX))
        ve(lambda e: e.tensor_single_scalar(gt[0:1, 16:24], gt[0:1, 0:8],
                                            gt[0:1, 8:9], AL.is_ge))
        ve(lambda e: e.scalar_tensor_tensor(gt[0:1, 16:24], gt[0:1, 16:24],
                                            -1048576.0, idxbig[0:1, :],
                                            AL.mult, AL.add))
        ve(lambda e: e.tensor_reduce(gt[0:1, 9:10], gt[0:1, 16:24], AX, AL.min))
        t_sel = ve(lambda e: e.tensor_copy(seli[0:1, 0:1], gt[0:1, 9:10]))

        p.wait("gpsimd", "vector", t_sel)
        p.ccnt["lg"] += 16 * 4

        def dyn_loads(e):
            sem = p.sems["lg"]
            with e.register("selr") as r, e.register("offr") as r2:
                e.reg_load(r, seli[0:1, 0:1])
                e.reg_mul(r2, r, 1024 * 2048)
                e.dma_start(out=b_w1[:, :],
                            in_=bass.AP(f0_W1, r2,
                                        [[2048, 128], [128 * 2048, 8], [1, 2048]])
                            ).then_inc(sem, 16)
                e.reg_mul(r2, r, 2048 * 1024)
                e.dma_start(out=b_w2[:, :],
                            in_=bass.AP(f0_W2, r2,
                                        [[1024, 128], [128 * 1024, 16], [1, 1024]])
                            ).then_inc(sem, 16)
                e.reg_mul(r2, r, 128 * 16)
                e.dma_start(out=f0b1s[:, :],
                            in_=bass.AP(f0_b1, r2, [[16, 128], [1, 16]])
                            ).then_inc(sem, 16)
                e.reg_mul(r2, r, 128 * 8)
                e.dma_start(out=f0b2s[:, :],
                            in_=bass.AP(f0_b2, r2, [[8, 128], [1, 8]])
                            ).then_inc(sem, 16)
            return e.nop()
        p.emit("gpsimd", dyn_loads, inc=0)
        dma("gpsimd", "lg", b_wo[:, :],
            bass.AP(f0_Wo, 0, [[1024, 128], [128 * 1024, 8], [1, 1024]]))
        t_wsel = p.ccnt["lg"]
        p.waitc("tensor", "lg", t_wsel)
        p.waitc("scalar", "lg", t_wsel)

        for g in range(8):
            t_lng = layernorm_cg(
                G2, lambda s, g=g: xap(s, g, G2),
                lambda s: b_yg[:, s * G2:(s + 1) * G2],
                [(lambda s: b_ybf[:, s * G2:(s + 1) * G2], None, None),
                 (lambda s: b_yg[:, s * G2:(s + 1) * G2], None, None)],
                b_xbf2, b_sq2, b_rsc, b_tmpln)
            p.wait("tensor", "vector", t_lng)
            for fc in range(16):
                bk = 2 + (fc % 2)
                mm_group(bk, [(b_w1[:, dc * 2048 + fc * 128:dc * 2048 + fc * 128 + 128],
                               b_ybf[:, dc * G2:(dc + 1) * G2]) for dc in range(NS)],
                         pb[bk][:, :G2])
                bank_read(bk, "scalar")
                se(lambda e, fc=fc, bk=bk: e.activation(
                    b_h1[:, fc * G2:(fc + 1) * G2], pb[bk][:, :G2], AF.Gelu,
                    bias=f0b1s[:, fc:fc + 1]))
                bank_rdone(bk, "scalar")
            t_h1p2 = p.tok("scalar")
            p.wait("tensor", "scalar", t_h1p2)
            for dc in range(NS):
                bk = 4 + (dc % 2)
                mm_group(bk, [(b_w2[:, fc * 1024 + dc * 128:fc * 1024 + dc * 128 + 128],
                               b_h1[:, fc * G2:(fc + 1) * G2]) for fc in range(16)],
                         pb[bk][:, :G2])
                bank_read(bk, "scalar")
                se(lambda e, dc=dc, bk=bk: e.activation(
                    b_rsn[:, dc * G2:(dc + 1) * G2], pb[bk][:, :G2], AF.Identity,
                    bias=f0b2s[:, dc:dc + 1]))
                bank_rdone(bk, "scalar")
            t_rsn = p.tok("scalar")
            p.wait("tensor", "scalar", t_rsn)
            for dc in range(NS):
                bk = 6 + (dc % 2)
                mm_group(bk, [(b_wo[:, kc * 1024 + dc * 128:kc * 1024 + dc * 128 + 128],
                               b_rsn[:, kc * G2:(kc + 1) * G2]) for kc in range(NS)],
                         pb[bk][:, :G2])
                bank_read(bk, "vector")
                ve(lambda e, dc=dc, bk=bk: e.scalar_tensor_tensor(
                    b_tmpln[:, :G2], pb[bk][:, :G2], 0.3,
                    b_yg[:, dc * G2:(dc + 1) * G2], AL.mult, AL.add))
                ve(lambda e, dc=dc, g=g: e.scalar_tensor_tensor(
                    xap(dc, g, G2), b_tmpln[:, :G2], f0bo3[:, dc:dc + 1],
                    xap(dc, g, G2), AL.add, AL.add))
                bank_rdone(bk, "vector")
        p.barrier()

    if phases >= 3:
        G3 = 512
        c_wqkv = sb("c_wqkv", (128, 24576), BF16, S0)
        c_wo1 = sb("c_wo1", (128, 8192), BF16, S0 + 49152)
        c_invt = sb("c_invt", (128, T), F32, S0 + 65536)
        c_ybf = sb("c_ybf", (128, 4096), BF16, S0 + 73728)
        c_qdm = sb("c_qdm", (128, 4096), BF16, S0 + 81920)
        c_kdm = sb("c_kdm", (128, 4096), BF16, S0 + 90112)
        c_xc = sb("c_xc", (128, 4096), F32, S0 + 81920)
        c_ktm = sb("c_ktm", (128, 4096), BF16, S0 + 98304)
        c_vtm = sb("c_vtm", (128, 4096), BF16, S0 + 106496)
        c_heads = sb("c_heads", (128, 4096), BF16, S0 + 73728)
        c_pref = sb("c_pref", (128, 512), F32, S0 + 114688)
        c_prbf = sb("c_prbf", (128, 64), BF16, S0 + 116736)
        c_sTb = sb("c_sTb", (128, 128), BF16, S0 + 116864)
        c_xbf2 = sb("c_xbf2", (128, 1024), BF16, S0 + 117024 - 32)
        c_sq2 = sb("c_sq2", (128, 1024), BF16, S0 + 119040)
        c_rsc = sb("c_rsc", (128, 512), F32, S0 + 121088)
        c_tmpln = c_rsc
        assert S0 + 123136 <= SB_END

        for dc in range(NS):
            dma("gpsimd", "lg", c_wqkv[:, dc * 3072:(dc + 1) * 3072],
                bass.AP(l1_wqkv, dc * 128 * 3072, [[3072, 128], [1, 3072]]))
        dma("gpsimd", "lg", c_wo1[:, :],
            bass.AP(l1_wo, 0, [[1024, 128], [128 * 1024, 8], [1, 1024]]))
        t_w3i = dma("sync", "ld", c_invt[:, :], c_invtot.ap())
        t_w3 = p.ccnt["lg"]
        p.waitc("tensor", "lg", t_w3)
        p.waitc("vector", "lg", t_w3)
        p.waitc("vector", "ld", t_w3i)
        tvz = ve(lambda e: e.memset(c_pref[:, :], 0.0))

        t_pref_use = [0, 0]
        t_sTb_use = [0, 0]
        for g in range(4):
            t_ln3 = layernorm_cg(
                G3, lambda s, g=g: xap(s, g, G3),
                lambda s: c_xc[:, s * G3:(s + 1) * G3],
                [(lambda s: c_ybf[:, s * G3:(s + 1) * G3], None, None)],
                c_xbf2, c_sq2, c_rsc, c_tmpln)
            p.wait("tensor", "vector", t_ln3)
            for o_ in range(NS):
                bk = o_ % 2
                mm_group(bk, [(c_wqkv[:, dc * 3072 + o_ * 128:dc * 3072 + o_ * 128 + 128],
                               c_ybf[:, dc * G3:(dc + 1) * G3]) for dc in range(NS)],
                         pb[bk][:, :G3])
                bank_read(bk, "scalar")
                se(lambda e, o_=o_, bk=bk: e.activation(
                    c_qdm[:, o_ * G3:(o_ + 1) * G3], pb[bk][:, :G3], AF.Identity))
                bank_rdone(bk, "scalar")
            for o_ in range(NS):
                bk = o_ % 2
                mm_group(bk, [(c_wqkv[:, dc * 3072 + 1024 + o_ * 128:
                                      dc * 3072 + 1024 + o_ * 128 + 128],
                               c_ybf[:, dc * G3:(dc + 1) * G3]) for dc in range(NS)],
                         pb[bk][:, :G3])
                bank_read(bk, "vector")
                ve(lambda e, o_=o_, bk=bk: e.tensor_scalar(
                    c_kdm[:, o_ * G3:(o_ + 1) * G3], pb[bk][:, :G3], 0.0, 0.5,
                    AL.is_ge, AL.subtract))
                bank_rdone(bk, "vector")
            for (dst, col0) in ((c_ktm, 1024), (c_vtm, 2048)):
                for tt in range(4):
                    for hf in range(2):
                        bk = (tt * 2 + hf) % 2
                        mm_group(bk, [(c_ybf[:, dc * G3 + tt * 128:dc * G3 + tt * 128 + 128],
                                       c_wqkv[:, dc * 3072 + col0 + hf * 512:
                                              dc * 3072 + col0 + hf * 512 + 512])
                                      for dc in range(NS)], pb[bk][:, :512])
                        bank_read(bk, "vector")
                        ve(lambda e, dst=dst, tt=tt, hf=hf, bk=bk: e.tensor_scalar(
                            dst[:, tt * 1024 + hf * 512:tt * 1024 + hf * 512 + 512],
                            pb[bk][:, :512], 0.0, 0.5, AL.is_ge, AL.subtract))
                        bank_rdone(bk, "vector")
            t_qkv = p.tok("vector")
            p.wait("tensor", "vector", t_qkv)
            p.wait("vector", "vector", 0)
            if g == 0:
                p.wait("tensor", "vector", tvz)

            for nl in range(8):
                for h in range(16):
                    par = h % 2
                    pof = par * 64
                    so = (h // 2) * G3 + nl * 64
                    to = (nl // 2) * 1024 + h * 64
                    tp = (nl % 2) * 64
                    p.wait("vector", "tensor", t_pref_use[par])
                    tpb = ve(lambda e, pof=pof, h=h: e.tensor_copy(
                        c_prbf[pof:pof + 64, :],
                        c_pref[pof:pof + 64, (h // 2) * 64:(h // 2) * 64 + 64]))
                    bk = 2 + par
                    mm_group(bk, [(c_kdm[pof:pof + 64, so:so + 64],
                                   c_qdm[pof:pof + 64, so:so + 64])],
                             pb[bk][tp:tp + 64, 0:64])
                    bank_read(bk, "vector")
                    p.wait("vector", "tensor", t_sTb_use[par])
                    tsb = ve(lambda e, tp=tp, par=par, bk=bk: e.tensor_mul(
                        c_sTb[tp:tp + 64, par * 64:par * 64 + 64],
                        pb[bk][tp:tp + 64, 0:64], maskT[tp:tp + 64, :]))
                    bank_rdone(bk, "vector")
                    bk2 = 4 + par
                    bank_prep(bk2)
                    p.wait("tensor", "vector", tpb)
                    mm(c_prbf[pof:pof + 64, :], c_qdm[pof:pof + 64, so:so + 64],
                       pb[bk2][pof:pof + 64, 0:64], start=True, stop=False)
                    p.wait("tensor", "vector", tsb)
                    mm(c_vtm[tp:tp + 64, to:to + 64],
                       c_sTb[tp:tp + 64, par * 64:par * 64 + 64],
                       pb[bk2][pof:pof + 64, 0:64], start=False, stop=True, inc=1)
                    t_pref_use[par] = p.tok("tensor")
                    t_sTb_use[par] = p.tok("tensor")
                    bank_done(bk2)
                    bank_read(bk2, "vector")
                    ve(lambda e, pof=pof, so=so, g=g, nl=nl, bk2=bk2: e.tensor_mul(
                        c_heads[pof:pof + 64, so:so + 64],
                        pb[bk2][pof:pof + 64, 0:64],
                        c_invt[pof:pof + 64, g * G3 + nl * 64:g * G3 + nl * 64 + 64]))
                    bank_rdone(bk2, "vector")
                    bk3 = 6 + par
                    mm_group(bk3, [(c_ktm[tp:tp + 64, to:to + 64],
                                    c_vtm[tp:tp + 64, to:to + 64])],
                             pb[bk3][pof:pof + 64, 0:64])
                    bank_read(bk3, "vector")
                    ve(lambda e, pof=pof, h=h, bk3=bk3: e.tensor_add(
                        c_pref[pof:pof + 64, (h // 2) * 64:(h // 2) * 64 + 64],
                        c_pref[pof:pof + 64, (h // 2) * 64:(h // 2) * 64 + 64],
                        pb[bk3][pof:pof + 64, 0:64]))
                    bank_rdone(bk3, "vector")
            t_hd3 = p.tok("vector")
            p.wait("tensor", "vector", t_hd3)
            for dc in range(NS):
                bk = dc % 2
                mm_group(bk, [(c_wo1[:, kc * 1024 + dc * 128:kc * 1024 + dc * 128 + 128],
                               c_heads[:, kc * G3:(kc + 1) * G3]) for kc in range(NS)],
                         pb[bk][:, :G3])
                bank_read(bk, "vector")
                ve(lambda e, dc=dc, g=g, bk=bk: e.tensor_add(
                    xap(dc, g, G3), xap(dc, g, G3), pb[bk][:, :G3]))
                bank_rdone(bk, "vector")
        p.barrier()

    if phases >= 4:
        d_ybf = sb("d_ybf", (128, 16384), BF16, S0)
        d_h2 = sb("d_h2", (128, 32768), BF16, S0 + 32768)
        d_xc = sb("d_xc", (128, 16384), F32, S0 + 32768)
        d_ye = sb("d_ye", (128, 8192), BF16, S0 + 98304)
        d_wd = sb("d_wd", (128, 8192), BF16, S0 + 98304)
        d_wg = sb("d_wg", (128, 2048), BF16, S0 + 114688)
        d_wu = sb("d_wu", (128, 2048), BF16, S0 + 118784)
        d_xbf2 = sb("d_xbf2", (128, 1024), BF16, S0 + 98304)
        d_sq2 = sb("d_sq2", (128, 1024), BF16, S0 + 100352)
        d_rsc = sb("d_rsc", (128, 512), F32, S0 + 102400)
        d_tmpln = d_rsc
        assert S0 + 122880 <= SB_END

        for cg in range(4):
            layernorm_cg(
                512, lambda s, cg=cg: x[:, s * T + cg * 512:s * T + cg * 512 + 512],
                lambda s, cg=cg: d_xc[:, s * 2048 + cg * 512:s * 2048 + cg * 512 + 512],
                [(lambda s, cg=cg: d_ybf[:, s * 2048 + cg * 512:s * 2048 + cg * 512 + 512],
                  None, None)],
                d_xbf2, d_sq2, d_rsc, d_tmpln)
        t_ln4 = p.tok("vector")
        p.wait("tensor", "vector", t_ln4)

        for tt in range(16):
            bk = tt % 2
            mm_group(bk, [(d_ybf[:, dc * 2048 + tt * 128:dc * 2048 + tt * 128 + 128],
                           gatebf[:, dc * 8:(dc + 1) * 8]) for dc in range(NS)],
                     pb[bk][:, 0:8])
            bank_read(bk, "vector")
            ve(lambda e, bk=bk: e.tensor_copy(gsc[:, 0:8], pb[bk][:, 0:8]))
            bank_rdone(bk, "vector")
            ve(lambda e: e.reduce_max(gsc[:, 32:33], gsc[:, 0:8], axis=AX))
            ve(lambda e: e.tensor_single_scalar(gsc[:, 8:16], gsc[:, 0:8],
                                                gsc[:, 32:33], AL.is_ge))
            ve(lambda e: e.scalar_tensor_tensor(gsc[:, 16:24], gsc[:, 8:16], -1e30,
                                                gsc[:, 0:8], AL.mult, AL.add))
            ve(lambda e: e.reduce_max(gsc[:, 33:34], gsc[:, 16:24], axis=AX))
            ve(lambda e: e.tensor_single_scalar(gsc[:, 24:32], gsc[:, 16:24],
                                                gsc[:, 33:34], AL.is_ge))
            tnv = ve(lambda e: e.tensor_scalar_mul(gsc[:, 34:35], gsc[:, 32:33], -1.0))
            p.wait("scalar", "vector", tnv)
            tp2 = se(lambda e: e.activation(gsc[:, 35:36], gsc[:, 33:34], AF.Sigmoid,
                                            bias=gsc[:, 34:35]))
            p.wait("vector", "scalar", tp2)
            ve(lambda e: e.tensor_scalar(gsc[:, 36:37], gsc[:, 35:36], -1.0, 1.0,
                                         AL.mult, AL.add))
            ve(lambda e: e.tensor_single_scalar(gsc[:, 40:48], gsc[:, 8:16],
                                                gsc[:, 36:37], AL.mult))
            tcb = ve(lambda e: e.scalar_tensor_tensor(gsc[:, 40:48], gsc[:, 24:32],
                                                      gsc[:, 35:36], gsc[:, 40:48],
                                                      AL.mult, AL.add))
            p.wait("tensor", "vector", tcb)
            bk2 = 2 + tt % 2
            bank_prep(bk2)
            p.emit("tensor", lambda e, bk2=bk2: e.transpose(
                pb[bk2][0:8, 0:128], gsc[:, 40:48], ident[:, :]), inc=1)
            bank_done(bk2)
            bank_read(bk2, "vector")
            ve(lambda e, tt=tt, bk2=bk2: e.tensor_copy(
                comb[:, tt * 128:(tt + 1) * 128], pb[bk2][0:8, 0:128]))
            bank_rdone(bk2, "vector")
        t_comb = p.tok("vector")
        p.wait("tensor", "vector", t_comb)

        for G in range(2):
            for e_ in range(8):
                for hf in range(2):
                    bk = 6 + hf
                    mm_group(bk, [(bcast8[:, e_ * 128:(e_ + 1) * 128],
                                   comb[:, G * 1024 + hf * 512:G * 1024 + hf * 512 + 512])],
                             pb[bk][:, :512])
                    bank_read(bk, "vector")
                    for s in range(NS):
                        ve(lambda e, s=s, hf=hf, G=G, bk=bk: e.tensor_mul(
                            d_ye[:, s * 1024 + hf * 512:s * 1024 + hf * 512 + 512],
                            d_ybf[:, s * 2048 + G * 1024 + hf * 512:
                                  s * 2048 + G * 1024 + hf * 512 + 512],
                            pb[bk][:, :512]))
                    bank_rdone(bk, "vector")
                t_ye = p.tok("vector")
                p.wait("tensor", "vector", t_ye)
                t_wguse = [0] * 32
                for fc in range(32):
                    par = fc % 2
                    if fc >= 2:
                        p.wait("gpsimd", "tensor", t_wguse[fc - 2])
                    dma("gpsimd", "w%d" % par, d_wg[:, par * 1024:par * 1024 + 1024],
                        bass.AP(f1_wg, e_ * D * 4096 + fc * 128,
                                [[4096, 128], [128 * 4096, 8], [1, 128]]))
                    tld = dma("gpsimd", "w%d" % par, d_wu[:, par * 1024:par * 1024 + 1024],
                              bass.AP(f1_wu, e_ * D * 4096 + fc * 128,
                                      [[4096, 128], [128 * 4096, 8], [1, 128]]))
                    p.waitc("tensor", "w%d" % par, tld)
                    for hf in range(2):
                        bkg = hf
                        mm_group(bkg, [(d_wg[:, par * 1024 + dc * 128:par * 1024 + dc * 128 + 128],
                                        d_ybf[:, dc * 2048 + G * 1024 + hf * 512:
                                              dc * 2048 + G * 1024 + hf * 512 + 512])
                                       for dc in range(NS)], pb[bkg][:, :512])
                        bku = 2 + hf
                        mm_group(bku, [(d_wu[:, par * 1024 + dc * 128:par * 1024 + dc * 128 + 128],
                                        d_ye[:, dc * 1024 + hf * 512:
                                             dc * 1024 + hf * 512 + 512])
                                       for dc in range(NS)], pb[bku][:, :512])
                    t_wguse[fc] = p.tok("tensor")
                    for hf in range(2):
                        bkg, bku = hf, 2 + hf
                        bank_read(bkg, "scalar")
                        tsl = se(lambda e, fc=fc, hf=hf, bkg=bkg: e.activation(
                            d_h2[:, fc * 1024 + hf * 512:fc * 1024 + hf * 512 + 512],
                            pb[bkg][:, :512], AF.Silu))
                        bank_rdone(bkg, "scalar")
                        p.wait("vector", "scalar", tsl)
                        bank_read(bku, "vector")
                        ve(lambda e, fc=fc, hf=hf, bku=bku: e.tensor_mul(
                            d_h2[:, fc * 1024 + hf * 512:fc * 1024 + hf * 512 + 512],
                            d_h2[:, fc * 1024 + hf * 512:fc * 1024 + hf * 512 + 512],
                            pb[bku][:, :512]))
                        bank_rdone(bku, "vector")
                t_h2 = p.tok("vector")
                p.wait("tensor", "vector", t_h2)
                p.wait("gpsimd", "vector", t_h2)
                t_wduse = [0] * NS
                for dc in range(NS):
                    par = dc % 2
                    if dc >= 2:
                        p.wait("gpsimd", "tensor", t_wduse[dc - 2])
                    tld = dma("gpsimd", "u%d" % par, d_wd[:, par * 4096:par * 4096 + 4096],
                              bass.AP(f1_wd, e_ * 4096 * D + dc * 128,
                                      [[1024, 128], [128 * 1024, 32], [1, 128]]))
                    p.waitc("tensor", "u%d" % par, tld)
                    for hf in range(2):
                        bk = 4 + hf
                        mm_group(bk, [(d_wd[:, par * 4096 + fc * 128:par * 4096 + fc * 128 + 128],
                                       d_h2[:, fc * 1024 + hf * 512:fc * 1024 + hf * 512 + 512])
                                      for fc in range(32)], pb[bk][:, :512])
                    t_wduse[dc] = p.tok("tensor")
                    for hf in range(2):
                        bk = 4 + hf
                        bank_read(bk, "vector")
                        ve(lambda e, dc=dc, hf=hf, G=G, bk=bk: e.tensor_add(
                            x[:, dc * T + G * 1024 + hf * 512:
                              dc * T + G * 1024 + hf * 512 + 512],
                            x[:, dc * T + G * 1024 + hf * 512:
                              dc * T + G * 1024 + hf * 512 + 512],
                            pb[bk][:, :512]))
                        bank_rdone(bk, "vector")
        p.barrier()

    p.barrier(engs=("sync",))
    tout = dma("sync", "out", bass.AP(out_d, 0, [[T, 128], [128 * T, NS], [1, T]]),
               x[:, :])
    p.emit("sync", lambda e, t=tout: e.wait_ge(p.sems["out"], t))

    p.finalize()
    return nc


LN_EPS_H = 1e-5
PI_2_H = math.pi / 2


def prep_inputs(inputs):
    f = np.float32
    c = np.ascontiguousarray
    WQ = inputs["L0_attn_W_Q"]; WK = inputs["L0_attn_W_K"]
    WV = inputs["L0_attn_W_V"]; WO = inputs["L0_attn_W_O"]
    base = {
        "l0_wq16": c(WQ[:, :, 0].T.astype(f)),
        "l0_wk16": c(WK[:, :, 0].T.astype(f)),
        "l0_wv": c(WV.transpose(1, 0, 2).reshape(1024, 1024).astype(f)),
        "l0_wo": c(WO.transpose(2, 1, 0).reshape(1024, 1024).astype(f)),
        "l0_ln1w": c(inputs["L0_attn_ln1_w"].reshape(8, 128).T.astype(f)),
        "l0_ln1b": c(inputs["L0_attn_ln1_b"].reshape(8, 128).T.astype(f)),
        "l0_ln2w": c(inputs["L0_attn_ln2_w"].reshape(8, 128).T.astype(f)),
        "l0_ln2b": c(inputs["L0_attn_ln2_b"].reshape(8, 128).T.astype(f)),
        "l0_W1": c(inputs["L0_attn_W1"].astype(f)),
        "l0_b1": c(inputs["L0_attn_b1"].reshape(32, 128).T.astype(f)),
        "l0_W2": c(inputs["L0_attn_W2"].astype(f)),
        "l0_b2": c(inputs["L0_attn_b2"].reshape(8, 128).T.astype(f)),
        "f0_Wp1": c(inputs["L0_ffn_Wp1"].astype(f)),
        "f0_bp1": c(inputs["L0_ffn_bp1"].reshape(128, 1).astype(f)),
        "f0_Wp2": c(inputs["L0_ffn_Wp2"].astype(f)),
        "f0_bp2": c(inputs["L0_ffn_bp2"].reshape(1, 8).astype(f)),
        "f0_W1": c(inputs["L0_ffn_W1"].astype(f)),
        "f0_b1": c(inputs["L0_ffn_b1"].reshape(8, 16, 128).transpose(0, 2, 1).astype(f)),
        "f0_W2": c(inputs["L0_ffn_W2"].astype(f)),
        "f0_b2": c(inputs["L0_ffn_b2"].reshape(8, 8, 128).transpose(0, 2, 1).astype(f)),
        "f0_Wo": c(inputs["L0_ffn_Wo"].astype(f)),
        "f0_bo": c(inputs["L0_ffn_bo"].reshape(8, 128).T.astype(f)),
        "l1_wqkv": c(inputs["L1_attn_W_qkv"].T.astype(f)),
        "l1_wo": c(inputs["L1_attn_W_o"].T.astype(f)),
        "f1_gate": c(inputs["L1_ffn_gate_w"].T.astype(f)),
        "f1_wg": c(inputs["L1_ffn_wg"].transpose(0, 2, 1).astype(f)),
        "f1_wu": c(inputs["L1_ffn_wu"].transpose(0, 2, 1).astype(f)),
        "f1_wd": c(inputs["L1_ffn_wd"].transpose(0, 2, 1).astype(f)),
        "c_ones_bf": np.ones((128, 128), dtype=ml_dtypes.bfloat16),
        "c_ones_f": np.ones((128, 128), dtype=f),
        "c_ident": np.eye(128, dtype=f),
        "c_idxbig": (np.arange(8, dtype=f) + 1048576.0).reshape(1, 8),
        "c_flt": c(np.broadcast_to(
            np.array([LN_EPS_H, PI_2_H], dtype=f), (128, 2)).astype(f)),
    }
    sel = np.zeros((16, 1024), dtype=f)
    for k in range(16):
        s, r = divmod(k, 2)
        sel[k, s * 128 + r * 64:s * 128 + r * 64 + 64] = 1.0
    base["c_sel16"] = sel
    bc8 = np.zeros((8, 1024), dtype=f)
    for e in range(8):
        bc8[e, e * 128:(e + 1) * 128] = 1.0
    base["c_bcast8"] = bc8
    mT = np.triu(np.ones((64, 64), dtype=f))
    base["c_maskT"] = c(np.concatenate([mT, mT], axis=0))
    base["c_invtot"] = c(np.broadcast_to(
        0.5 / (np.arange(T, dtype=np.float64) + 1.0), (128, T)).astype(f))
    xs = inputs["L0_attn_x"]
    maps = []
    for b in range(8):
        m = dict(base)
        m["x_in"] = c(xs[b].T.astype(f))
        maps.append(m)
    return maps


def kernel(**inputs):
    maps = prep_inputs(inputs)
    nc = build_core_program(4)
    res = run_bass_kernel_spmd(nc, maps, core_ids=list(range(8)))
    out = np.stack([np.asarray(r["out"]).T for r in res.results], axis=0)
    return out.astype(np.float32)


# version 02
# speedup vs baseline: 1.0373x; 1.0373x over previous
import math
import contextlib
import numpy as np
import ml_dtypes
import concourse.bass as bass
import concourse.mybir as mybir
from concourse.bass_utils import run_bass_kernel_spmd

F32 = mybir.dt.float32
BF16 = mybir.dt.bfloat16
I32 = mybir.dt.int32
AF = mybir.ActivationFunctionType
AL = mybir.AluOpType
AX = mybir.AxisListType.X

D = 1024
T = 2048
NS = 8
LN_EPS = 1e-5
COS_EPS = 1e-6
PI = math.pi

ENGS = ("tensor", "vector", "scalar", "gpsimd", "sync")
CHANNELS = ("ld", "lg", "w0", "w1", "u0", "u1", "out")


class Prog:

    def __init__(self, nc):
        self.nc = nc
        self.ops = {e: [] for e in ENGS}
        self.cnt = {e: 0 for e in ENGS}
        self.ccnt = {c: 0 for c in CHANNELS}
        self.sems = {}

    def emit(self, eng, fn, inc=0):
        self.ops[eng].append((fn, inc, eng in ("vector", "scalar")))
        if inc:
            self.cnt[eng] += inc
        return self.cnt[eng]

    def wait(self, eng, prod, tok):
        if tok <= 0 or prod == eng:
            return
        self.ops[eng].append(
            (lambda e, prod=prod, tok=tok: e.wait_ge(self.sems[prod], tok), 0, False))

    def waitc(self, eng, ch, tok):
        if tok <= 0:
            return
        self.ops[eng].append(
            (lambda e, ch=ch, tok=tok: e.wait_ge(self.sems[ch], tok), 0, False))

    def tok(self, eng):
        return self.cnt[eng]

    def barrier(self, engs=("tensor", "vector", "scalar", "gpsimd", "sync")):
        toks = {e: self.cnt[e] for e in ENGS}
        ctoks = {c: self.ccnt[c] for c in CHANNELS}
        for e in engs:
            for prod in ENGS:
                if prod != e and toks[prod] > 0:
                    self.wait(e, prod, toks[prod])
            for c in CHANNELS:
                if ctoks[c] > 0:
                    self.waitc(e, c, ctoks[c])

    def finalize(self):
        nc = self.nc
        with contextlib.ExitStack() as st:
            for e in ENGS:
                self.sems[e] = st.enter_context(nc.semaphore(f"sem_{e}"))
            for c in CHANNELS:
                self.sems[c] = st.enter_context(nc.semaphore(f"ch_{c}"))
            with nc.Block() as block:
                for e in ENGS:
                    def run(proxy, e=e):
                        sem = self.sems[e]
                        for fn, inc, dr in self.ops[e]:
                            inst = fn(proxy)
                            if inc:
                                inst.then_inc(sem, inc)
                            if dr:
                                proxy.drain()
                    getattr(block, e)(run)


def build_core_program(phases=4):
    nc = bass.Bass("TRN2", target_bir_lowering=False)

    def din(name, shape, dtype=F32):
        return nc.dram_tensor(name, list(shape), dtype, kind="ExternalInput")

    x_in = din("x_in", (D, T))
    l0_wq16 = din("l0_wq16", (D, 16))
    l0_wk16 = din("l0_wk16", (D, 16))
    l0_wv = din("l0_wv", (D, D))
    l0_wo = din("l0_wo", (D, D))
    l0_ln1w = din("l0_ln1w", (128, 8)); l0_ln1b = din("l0_ln1b", (128, 8))
    l0_ln2w = din("l0_ln2w", (128, 8)); l0_ln2b = din("l0_ln2b", (128, 8))
    l0_W1 = din("l0_W1", (D, 4096)); l0_b1 = din("l0_b1", (128, 32))
    l0_W2 = din("l0_W2", (4096, D)); l0_b2 = din("l0_b2", (128, 8))
    f0_Wp1 = din("f0_Wp1", (D, 128)); f0_bp1 = din("f0_bp1", (128, 1))
    f0_Wp2 = din("f0_Wp2", (128, 8)); f0_bp2 = din("f0_bp2", (1, 8))
    f0_W1 = din("f0_W1", (8, D, 2048)); f0_b1 = din("f0_b1", (8, 128, 16))
    f0_W2 = din("f0_W2", (8, 2048, D)); f0_b2 = din("f0_b2", (8, 128, 8))
    f0_Wo = din("f0_Wo", (D, D)); f0_bo = din("f0_bo", (128, 8))
    l1_wqkv = din("l1_wqkv", (D, 3072))
    l1_wo = din("l1_wo", (D, D))
    f1_gate = din("f1_gate", (D, 8))
    f1_wg = din("f1_wg", (8, D, 4096))
    f1_wu = din("f1_wu", (8, D, 4096))
    f1_wd = din("f1_wd", (8, 4096, D))
    c_ones_bf = din("c_ones_bf", (128, 128), BF16)
    c_ones_f = din("c_ones_f", (128, 128))
    c_ident = din("c_ident", (128, 128))
    c_sel16 = din("c_sel16", (16, 1024))
    c_bcast8 = din("c_bcast8", (8, 1024))
    c_maskT = din("c_maskT", (128, 64))
    c_invtot = din("c_invtot", (128, T))
    c_idxbig = din("c_idxbig", (1, 8))
    c_flt = din("c_flt", (128, 2))
    out_d = nc.dram_tensor("out", [D, T], F32, kind="ExternalOutput")

    SB_BASE = 16512

    def sb(name, shape, dtype, off):
        assert off % 32 == 0, (name, off)
        return nc.alloc_sbuf_tensor_at(name, list(shape), dtype,
                                       offset=off + SB_BASE)

    SB_END = 212832
    x = sb("x", (128, NS * T), F32, 0)
    o = 65536
    def psb(name, shape, dtype, nbytes):
        nonlocal o
        t = sb(name, shape, dtype, o)
        o += nbytes
        return t
    ones_bf = psb("ones_bf", (128, 128), BF16, 256)
    ones_f = psb("ones_f", (128, 128), F32, 512)
    ident = psb("ident", (128, 128), F32, 512)
    ln1w = psb("ln1w", (128, 8), F32, 32)
    ln1b = psb("ln1b", (128, 8), F32, 32)
    ln2w = psb("ln2w", (128, 8), F32, 32)
    ln2b = psb("ln2b", (128, 8), F32, 32)
    b1sb = psb("b1sb", (128, 32), F32, 128)
    b2sb = psb("b2sb", (128, 8), F32, 32)
    wp1sb = psb("wp1sb", (128, 1024), F32, 4096)
    bp1sb = psb("bp1sb", (128, 1), F32, 32)
    wp2sb = psb("wp2sb", (128, 8), F32, 32)
    bp2sb = psb("bp2sb", (1, 8), F32, 32)
    f0b1s = psb("f0b1s", (128, 16), F32, 64)
    f0b2s = psb("f0b2s", (128, 8), F32, 32)
    f0bos = psb("f0bos", (128, 8), F32, 32)
    f0bo3 = psb("f0bo3", (128, 8), F32, 32)
    sel16 = psb("sel16", (16, 1024), F32, 4096)
    bcast8 = psb("bcast8", (8, 1024), F32, 4096)
    maskT = psb("maskT", (128, 64), F32, 256)
    idxbig = psb("idxbig", (1, 8), F32, 32)
    gatebf = psb("gatebf", (128, 64), BF16, 128)
    comb = psb("comb", (8, T), F32, 8192)
    carry_ck = psb("carry_ck", (128, 8), F32, 32)
    carry_sk = psb("carry_sk", (128, 8), F32, 32)
    carry16 = psb("carry16", (16, 4), F32, 32)
    gsc = psb("gsc", (128, 64), F32, 256)
    gt = psb("gt", (1, 32), F32, 128)
    seli = psb("seli", (1, 2), I32, 32)
    cflt = psb("cflt", (128, 2), F32, 32)
    nc.const_aps.register(F32, float(LN_EPS), cflt[:, 0:1])
    nc.const_aps.register(F32, float(PI / 2), cflt[:, 1:2])
    assert o <= 89504, o
    S0 = 89504

    pb = [nc.alloc_psum_tensor(f"pb{i}", [128, 512], F32) for i in range(8)]

    p = Prog(nc)

    bank_w = [0] * 8
    bank_r = [[] for _ in range(8)]

    def bank_prep(i):
        for (eng, tok) in bank_r[i]:
            p.wait("tensor", eng, tok)
        bank_r[i] = []

    def bank_done(i):
        bank_w[i] = p.tok("tensor")

    def bank_read(i, eng):
        p.wait(eng, "tensor", bank_w[i])

    def bank_rdone(i, eng):
        bank_r[i].append((eng, p.tok(eng)))

    def dma(queue, ch, out_ap, in_ap):
        def f(e, a=out_ap, b=in_ap, ch=ch):
            return e.dma_start(out=a, in_=b).then_inc(p.sems[ch], 16)
        p.emit(queue, f, inc=0)
        p.ccnt[ch] += 16
        return p.ccnt[ch]

    def mm(lhsT, rhs, out, start, stop, inc=0):
        return p.emit("tensor",
                      lambda e, a=lhsT, b=rhs, c=out, s1=start, s2=stop:
                      e.matmul(c, a, b, start=s1, stop=s2), inc=inc)

    def ve(fn, inc=1):
        return p.emit("vector", fn, inc=inc)

    def se(fn, inc=1):
        return p.emit("scalar", fn, inc=inc)

    def mm_group(bank_i, lhsT_rhs_list, out):
        bank_prep(bank_i)
        n = len(lhsT_rhs_list)
        for i, (a, b) in enumerate(lhsT_rhs_list):
            mm(a, b, out, start=(i == 0), stop=(i == n - 1),
               inc=(1 if i == n - 1 else 0))
        bank_done(bank_i)

    def layernorm_cg(Wc, src, xc, outs, sq2, rsc, tmpln):
        p.wait("tensor", "vector", p.tok("vector"))
        bank_prep(0)
        for s in range(NS):
            mm(ones_f[:, :], src(s), pb[0][:, :Wc],
               start=(s == 0), stop=(s == NS - 1), inc=(1 if s == NS - 1 else 0))
        bank_done(0)
        bank_read(0, "vector")
        for s in range(NS):
            ve(lambda e, a=xc(s), b=src(s):
               e.scalar_tensor_tensor(a, pb[0][:, :Wc], -1.0 / 1024.0, b,
                                      AL.mult, AL.add))
        xc_tok = p.tok("vector")
        bank_rdone(0, "vector")
        mm2 = [0] * NS
        bank_prep(1)
        p.wait("scalar", "vector", xc_tok)
        for s in range(NS):
            if s >= 2:
                p.wait("scalar", "tensor", mm2[s - 2])
            buf = sq2[:, (s % 2) * Wc:(s % 2) * Wc + Wc]
            ts = se(lambda e, a=buf, b=xc(s): e.activation(a, b, AF.Square))
            p.wait("tensor", "scalar", ts)
            mm2[s] = mm(ones_f[:, :], buf, pb[1][:, :Wc],
                        start=(s == 0), stop=(s == NS - 1), inc=1)
        bank_done(1)
        bank_read(1, "scalar")
        tsq = se(lambda e: e.activation(rsc[:, :Wc], pb[1][:, :Wc], AF.Sqrt,
                                        bias=float(LN_EPS), scale=1.0 / 1024.0))
        bank_rdone(1, "scalar")
        p.wait("vector", "scalar", tsq)
        ve(lambda e: e.reciprocal(rsc[:, :Wc], rsc[:, :Wc]))
        for s in range(NS):
            for (out_ap, wcol, bcol) in outs:
                if wcol is None:
                    ve(lambda e, a=out_ap(s), b=xc(s):
                       e.tensor_mul(a, b, rsc[:, :Wc]))
                else:
                    ve(lambda e, b=xc(s): e.tensor_mul(tmpln[:, :Wc], b, rsc[:, :Wc]))
                    ve(lambda e, a=out_ap(s), w=wcol(s), bb=bcol(s):
                       e.tensor_scalar(a, tmpln[:, :Wc], w, bb, AL.mult, AL.add))
        return p.tok("vector")

    def xap(s, g, W):
        return x[:, s * T + g * W: s * T + g * W + W]

    dma("sync", "ld", x[:, :], bass.AP(x_in, 0, [[T, 128], [128 * T, NS], [1, T]]))
    for (sbuf, dram) in [(ones_bf, c_ones_bf), (ones_f, c_ones_f),
                         (ident, c_ident), (ln1w, l0_ln1w), (ln1b, l0_ln1b),
                         (ln2w, l0_ln2w), (ln2b, l0_ln2b), (b1sb, l0_b1),
                         (b2sb, l0_b2), (bp1sb, f0_bp1), (wp2sb, f0_Wp2),
                         (bp2sb, f0_bp2), (f0bos, f0_bo), (sel16, c_sel16),
                         (bcast8, c_bcast8), (maskT, c_maskT),
                         (idxbig, c_idxbig), (cflt, c_flt)]:
        dma("sync", "ld", sbuf[:, :], dram.ap())
    dma("sync", "ld", wp1sb[:, :],
        bass.AP(f0_Wp1, 0, [[128, 128], [128 * 128, 8], [1, 128]]))
    dma("gpsimd", "lg", gatebf[:, :],
        bass.AP(f1_gate, 0, [[8, 128], [8 * 128, 8], [1, 8]]))

    if phases >= 1:
        G1 = 512
        a_y0 = sb("a_y0", (128, 8 * G1), F32, S0)
        a_y1 = sb("a_y1", (128, 8 * G1), BF16, S0 + 16384)
        a_wkrep = sb("a_wkrep", (128, 8192), BF16, S0 + 24576)
        a_wv = sb("a_wv", (128, 8192), BF16, S0 + 40960)
        a_h1 = sb("a_h1", (128, 16384), BF16, S0 + 24576)
        a_wo = sb("a_wo", (128, 8192), BF16, S0 + 57344)
        a_ckv = sb("a_ckv", (128, 8 * G1), F32, S0 + 73728)
        a_xc2 = sb("a_xc2", (128, 8 * G1), F32, S0 + 73728)
        a_skv = sb("a_skv", (128, 8 * G1), F32, S0 + 90112)
        a_w1c = sb("a_w1c", (128, 2048), BF16, S0 + 90112)
        a_w2c = sb("a_w2c", (128, 2048), BF16, S0 + 94208)
        a_heads = sb("a_heads", (128, 8 * G1), BF16, S0 + 16384)
        a_q16 = sb("a_q16", (16, 2560), F32, S0 + 106496)
        a_sq2 = sb("a_sq2", (128, 1024), F32, S0 + 106496)
        a_tk = sb("a_tk", (128, 512), F32, S0 + 116736)
        a_tmpln = sb("a_tmpln", (128, 512), F32, S0 + 116736)
        a_tmp1 = sb("a_tmp1", (128, 512), F32, S0 + 118784)
        a_rsc = sb("a_rsc", (128, 512), F32, S0 + 118784)
        a_wq16 = sb("a_wq16", (128, 256), BF16, S0 + 120832)
        assert S0 + 121344 <= SB_END

        dma("gpsimd", "lg", a_wq16[:, 0:128],
            bass.AP(l0_wq16, 0, [[16, 128], [16 * 128, 8], [1, 16]]))
        dma("gpsimd", "lg", a_wq16[:, 128:256],
            bass.AP(l0_wk16, 0, [[16, 128], [16 * 128, 8], [1, 16]]))
        dma("gpsimd", "lg", a_wo[:, :],
            bass.AP(l0_wo, 0, [[1024, 128], [128 * 1024, 8], [1, 1024]]))
        dma("gpsimd", "lg", a_wv[:, :],
            bass.AP(l0_wv, 0, [[1024, 128], [128 * 1024, 8], [1, 1024]]))
        t_ld0 = p.ccnt["ld"]
        t_lg0 = p.ccnt["lg"]
        for e in ("vector", "scalar", "tensor", "gpsimd"):
            p.waitc(e, "ld", t_ld0)
            p.waitc(e, "lg", t_lg0)
        ve(lambda e: e.tensor_scalar_mul(f0bo3[:, :], f0bos[:, :], 0.3))
        ve(lambda e: e.memset(carry_ck[:, :], 0.0))
        ve(lambda e: e.memset(carry_sk[:, :], 0.0))
        ve(lambda e: e.memset(carry16[:, :], 0.0))

        def q16s(i):
            return a_q16[:, i * 512:(i + 1) * 512]

        for g in range(4):
            if g > 0:
                p.barrier(engs=("gpsimd",))
                dma("gpsimd", "lg", a_wv[:, :],
                    bass.AP(l0_wv, 0, [[1024, 128], [128 * 1024, 8], [1, 1024]]))
            twv = p.ccnt["lg"]
            p.waitc("vector", "lg", twv)
            for dc in range(NS):
                ve(lambda e, dc=dc: e.tensor_copy(
                    a_wkrep[:, dc * 1024:(dc + 1) * 1024],
                    bass.AP(a_wq16, 128 + dc * 16, [[256, 128], [1, 16], [0, 64]])))
            twk = p.tok("vector")
            p.wait("tensor", "vector", twk)
            p.waitc("tensor", "lg", twv)

            layernorm_cg(G1, lambda s, g=g: xap(s, g, G1),
                         lambda s: a_y0[:, s * G1:(s + 1) * G1],
                         [(lambda s: a_y0[:, s * G1:(s + 1) * G1], None, None)],
                         a_sq2, a_rsc, a_tmpln)
            t_ln1 = layernorm_cg(
                G1, lambda s: a_y0[:, s * G1:(s + 1) * G1],
                lambda s: a_xc2[:, s * G1:(s + 1) * G1],
                [(lambda s: a_y1[:, s * G1:(s + 1) * G1],
                  lambda s: ln1w[:, s:s + 1], lambda s: ln1b[:, s:s + 1])],
                a_sq2, a_rsc, a_tmpln)
            p.wait("tensor", "vector", t_ln1)

            mm_group(2, [(a_wq16[:, dc * 16:dc * 16 + 16],
                          a_y1[:, dc * G1:(dc + 1) * G1]) for dc in range(NS)],
                     pb[2][0:16, :G1])
            mm_group(3, [(a_wq16[:, 128 + dc * 16:128 + dc * 16 + 16],
                          a_y1[:, dc * G1:(dc + 1) * G1]) for dc in range(NS)],
                     pb[3][0:16, :G1])
            bank_read(2, "scalar"); bank_read(3, "scalar")
            se(lambda e: e.activation(q16s(0)[0:16, :], pb[2][0:16, :G1], AF.Tanh))
            se(lambda e: e.activation(q16s(1)[0:16, :], q16s(0)[0:16, :], AF.Sin,
                                      bias=PI / 2, scale=PI / 4))
            se(lambda e: e.activation(q16s(2)[0:16, :], q16s(0)[0:16, :], AF.Sin,
                                      scale=PI / 4))
            se(lambda e: e.activation(q16s(0)[0:16, :], pb[3][0:16, :G1], AF.Tanh))
            se(lambda e: e.activation(q16s(3)[0:16, :], q16s(0)[0:16, :], AF.Sin,
                                      bias=PI / 2, scale=PI / 4))
            t_k16 = se(lambda e: e.activation(q16s(4)[0:16, :], q16s(0)[0:16, :],
                                              AF.Sin, scale=PI / 4))
            bank_rdone(2, "scalar"); bank_rdone(3, "scalar")
            p.wait("vector", "scalar", t_k16)
            for (i, cc) in ((3, 0), (4, 1)):
                ve(lambda e, i=i, cc=cc: e.tensor_tensor_scan(
                    q16s(i)[0:16, :], q16s(i)[0:16, :], q16s(i)[0:16, :],
                    carry16[0:16, cc:cc + 1], AL.add, AL.bypass))
                ve(lambda e, i=i, cc=cc: e.tensor_copy(
                    carry16[0:16, cc:cc + 1], a_q16[0:16, i * 512 + 511:i * 512 + 512]))
            ve(lambda e: e.tensor_mul(q16s(3)[0:16, :], q16s(1)[0:16, :], q16s(3)[0:16, :]))
            ve(lambda e: e.tensor_mul(q16s(4)[0:16, :], q16s(2)[0:16, :], q16s(4)[0:16, :]))
            ve(lambda e: e.tensor_add(q16s(3)[0:16, :], q16s(3)[0:16, :], q16s(4)[0:16, :]))
            ve(lambda e: e.tensor_scalar_add(q16s(3)[0:16, :], q16s(3)[0:16, :], COS_EPS))
            ve(lambda e: e.reciprocal(q16s(3)[0:16, :], q16s(3)[0:16, :]))
            ve(lambda e: e.tensor_mul(q16s(1)[0:16, :], q16s(1)[0:16, :], q16s(3)[0:16, :]))
            t_qd = ve(lambda e: e.tensor_mul(q16s(2)[0:16, :], q16s(2)[0:16, :],
                                             q16s(3)[0:16, :]))

            for s in range(NS):
                bk = 4 + (s % 2)
                mm_group(bk, [(a_wkrep[:, dc * 1024 + s * 128:dc * 1024 + s * 128 + 128],
                               a_y1[:, dc * G1:(dc + 1) * G1]) for dc in range(NS)],
                         pb[bk][:, :G1])
                bv_ = 6 + (s % 2)
                mm_group(bv_, [(a_wv[:, dc * 1024 + s * 128:dc * 1024 + s * 128 + 128],
                                a_y1[:, dc * G1:(dc + 1) * G1]) for dc in range(NS)],
                         pb[bv_][:, :G1])
                bank_read(bk, "scalar")
                se(lambda e, bk=bk: e.activation(a_tk[:, :], pb[bk][:, :G1], AF.Tanh))
                se(lambda e, s=s: e.activation(a_ckv[:, s * G1:(s + 1) * G1],
                                               a_tk[:, :], AF.Sin, bias=PI / 2,
                                               scale=PI / 4))
                tt = se(lambda e, s=s: e.activation(a_skv[:, s * G1:(s + 1) * G1],
                                                    a_tk[:, :], AF.Sin, scale=PI / 4))
                bank_rdone(bk, "scalar")
                p.wait("vector", "scalar", tt)
                bank_read(bv_, "vector")
                ve(lambda e, s=s, bv_=bv_: e.tensor_mul(
                    a_ckv[:, s * G1:(s + 1) * G1], a_ckv[:, s * G1:(s + 1) * G1],
                    pb[bv_][:, :G1]))
                ve(lambda e, s=s, bv_=bv_: e.tensor_mul(
                    a_skv[:, s * G1:(s + 1) * G1], a_skv[:, s * G1:(s + 1) * G1],
                    pb[bv_][:, :G1]))
                bank_rdone(bv_, "vector")
                for (buf, car) in ((a_ckv, carry_ck), (a_skv, carry_sk)):
                    ve(lambda e, s=s, buf=buf, car=car: e.tensor_tensor_scan(
                        buf[:, s * G1:(s + 1) * G1], buf[:, s * G1:(s + 1) * G1],
                        buf[:, s * G1:(s + 1) * G1], car[:, s:s + 1], AL.add, AL.bypass))
                    ve(lambda e, s=s, buf=buf, car=car: e.tensor_copy(
                        car[:, s:s + 1], buf[:, s * G1 + G1 - 1:s * G1 + G1]))

            p.wait("tensor", "vector", t_qd)
            for s in range(NS):
                bc, bs_ = (2, 3) if s % 2 == 0 else (4, 5)
                mm_group(bc, [(sel16[:, s * 128:(s + 1) * 128], q16s(1)[0:16, :])],
                         pb[bc][:, :G1])
                mm_group(bs_, [(sel16[:, s * 128:(s + 1) * 128], q16s(2)[0:16, :])],
                         pb[bs_][:, :G1])
                bank_read(bc, "vector"); bank_read(bs_, "vector")
                tsl = a_tmp1 if s % 2 == 0 else a_tk
                ve(lambda e, s=s, bc=bc: e.tensor_mul(
                    a_heads[:, s * G1:(s + 1) * G1],
                    a_ckv[:, s * G1:(s + 1) * G1], pb[bc][:, :G1]))
                ve(lambda e, s=s, bs_=bs_, tsl=tsl: e.tensor_mul(
                    tsl[:, :], a_skv[:, s * G1:(s + 1) * G1], pb[bs_][:, :G1]))
                ve(lambda e, s=s, tsl=tsl: e.tensor_add(
                    a_heads[:, s * G1:(s + 1) * G1],
                    a_heads[:, s * G1:(s + 1) * G1], tsl[:, :]))
                bank_rdone(bc, "vector"); bank_rdone(bs_, "vector")
            t_heads = p.tok("vector")
            p.wait("tensor", "vector", t_heads)

            for dc in range(NS):
                bk = dc % 2
                mm_group(bk, [(a_wo[:, kc * 1024 + dc * 128:kc * 1024 + dc * 128 + 128],
                               a_heads[:, kc * G1:(kc + 1) * G1]) for kc in range(NS)],
                         pb[bk][:, :G1])
                bank_read(bk, "vector")
                ve(lambda e, dc=dc, bk=bk: e.tensor_add(
                    a_y0[:, dc * G1:(dc + 1) * G1], a_y0[:, dc * G1:(dc + 1) * G1],
                    pb[bk][:, :G1]))
                ve(lambda e, dc=dc, g=g: e.tensor_add(
                    xap(dc, g, G1), xap(dc, g, G1), a_y0[:, dc * G1:(dc + 1) * G1]))
                bank_rdone(bk, "vector")

            t_ln2 = layernorm_cg(
                G1, lambda s: a_y0[:, s * G1:(s + 1) * G1],
                lambda s: a_y0[:, s * G1:(s + 1) * G1],
                [(lambda s: a_y1[:, s * G1:(s + 1) * G1],
                  lambda s: ln2w[:, s:s + 1], lambda s: ln2b[:, s:s + 1])],
                a_sq2, a_rsc, a_tmpln)
            p.wait("tensor", "vector", t_ln2)
            p.barrier(engs=("gpsimd",))

            t_w1use = [0] * 32
            for fc in range(32):
                half = fc % 2
                if fc >= 2:
                    p.wait("gpsimd", "tensor", t_w1use[fc - 2])
                tld = dma("gpsimd", "w%d" % half, a_w1c[:, half * 1024:half * 1024 + 1024],
                          bass.AP(l0_W1, fc * 128, [[4096, 128], [128 * 4096, 8], [1, 128]]))
                p.waitc("tensor", "w%d" % half, tld)
                bk = 2 + half
                mm_group(bk, [(a_w1c[:, half * 1024 + dc * 128:half * 1024 + dc * 128 + 128],
                               a_y1[:, dc * G1:(dc + 1) * G1]) for dc in range(NS)],
                         pb[bk][:, :G1])
                t_w1use[fc] = p.tok("tensor")
                bank_read(bk, "scalar")
                se(lambda e, fc=fc, bk=bk: e.activation(
                    a_h1[:, fc * G1:(fc + 1) * G1], pb[bk][:, :G1],
                    AF.Gelu_apprx_tanh, bias=b1sb[:, fc:fc + 1]))
                bank_rdone(bk, "scalar")
            t_h1 = p.tok("scalar")
            p.wait("tensor", "scalar", t_h1)

            for i in range(NS):
                bank_prep(i)
            t_w2use = [0] * 32
            for fc in range(32):
                half = fc % 2
                if fc >= 2:
                    p.wait("gpsimd", "tensor", t_w2use[fc - 2])
                tld = dma("gpsimd", "u%d" % half, a_w2c[:, half * 1024:half * 1024 + 1024],
                          bass.AP(l0_W2, fc * 128 * 1024, [[1024, 128], [1, 1024]]))
                p.waitc("tensor", "u%d" % half, tld)
                for dc in range(NS):
                    mm(a_w2c[:, half * 1024 + dc * 128:half * 1024 + dc * 128 + 128],
                       a_h1[:, fc * G1:(fc + 1) * G1], pb[dc][:, :G1],
                       start=(fc == 0), stop=(fc == 31),
                       inc=(1 if dc == NS - 1 else 0))
                t_w2use[fc] = p.tok("tensor")
            for dc in range(NS):
                bank_done(dc)
            t_down = p.tok("tensor")
            p.wait("vector", "tensor", t_down)
            for dc in range(NS):
                ve(lambda e, dc=dc, g=g: e.scalar_tensor_tensor(
                    xap(dc, g, G1), pb[dc][:, :G1], b2sb[:, dc:dc + 1],
                    xap(dc, g, G1), AL.add, AL.add))
                bank_rdone(dc, "vector")
            p.barrier()

    if phases >= 2:
        G2 = 256
        b_w1 = sb("b_w1", (128, 16384), BF16, S0)
        b_w2 = sb("b_w2", (128, 16384), BF16, S0 + 32768)
        b_wo = sb("b_wo", (128, 8192), BF16, S0 + 65536)
        b_yg = sb("b_yg", (128, 8 * G2), F32, S0 + 81920)
        b_ybf = sb("b_ybf", (128, 8 * G2), BF16, S0 + 90112)
        b_h1 = sb("b_h1", (128, 16 * G2), BF16, S0 + 94208)
        b_rsn = sb("b_rsn", (128, 8 * G2), BF16, S0 + 102400)
        b_sq2 = sb("b_sq2", (128, 512), F32, S0 + 106496)
        b_rsc = sb("b_rsc", (128, 256), F32, S0 + 108544)
        b_tmpln = sb("b_tmpln", (128, 256), F32, S0 + 109568)
        assert S0 + 110592 <= SB_END

        xs = x[:, :].rearrange("p (s t) -> p s t", t=T)
        xl = xs[:, :, 2047:2048].rearrange("p s o -> p (s o)")
        tvl = ve(lambda e: e.tensor_copy(gsc[:, 0:8], xl))
        p.wait("tensor", "vector", tvl)
        mm_group(0, [(ones_f[:, :], gsc[:, 0:8])], pb[0][:, 0:8])
        bank_read(0, "vector")
        ve(lambda e: e.reduce_sum(gsc[:, 24:25], pb[0][:, 0:8], axis=AX))
        bank_rdone(0, "vector")
        ve(lambda e: e.tensor_scalar_mul(gsc[:, 24:25], gsc[:, 24:25], 1.0 / 1024.0))
        tvc = ve(lambda e: e.tensor_single_scalar(gsc[:, 8:16], gsc[:, 0:8],
                                                  gsc[:, 24:25], AL.subtract))
        p.wait("scalar", "vector", tvc)
        tsc = se(lambda e: e.activation(gsc[:, 16:24], gsc[:, 8:16], AF.Square))
        p.wait("tensor", "scalar", tsc)
        mm_group(1, [(ones_f[:, :], gsc[:, 16:24])], pb[1][:, 0:8])
        bank_read(1, "vector")
        ve(lambda e: e.reduce_sum(gsc[:, 25:26], pb[1][:, 0:8], axis=AX))
        bank_rdone(1, "vector")
        tvv = p.tok("vector")
        p.wait("scalar", "vector", tvv)
        tss = se(lambda e: e.activation(gsc[:, 26:27], gsc[:, 25:26], AF.Sqrt,
                                        bias=float(LN_EPS), scale=1.0 / 1024.0))
        p.wait("vector", "scalar", tss)
        ve(lambda e: e.reciprocal(gsc[:, 26:27], gsc[:, 26:27]))
        tyl = ve(lambda e: e.tensor_single_scalar(gsc[:, 32:40], gsc[:, 8:16],
                                                  gsc[:, 26:27], AL.mult))
        p.wait("tensor", "vector", tyl)
        mm_group(2, [(wp1sb[:, dc * 128:(dc + 1) * 128], gsc[:, 32 + dc:33 + dc])
                     for dc in range(NS)], pb[2][:, 0:1])
        bank_read(2, "scalar")
        thp = se(lambda e: e.activation(gsc[:, 27:28], pb[2][:, 0:1], AF.Gelu,
                                        bias=bp1sb[:, 0:1]))
        bank_rdone(2, "scalar")
        p.wait("tensor", "scalar", thp)
        mm_group(3, [(gsc[:, 27:28], wp2sb[:, :])], pb[3][0:1, 0:8])
        bank_read(3, "vector")
        ve(lambda e: e.tensor_add(gt[0:1, 0:8], pb[3][0:1, 0:8], bp2sb[0:1, :]))
        bank_rdone(3, "vector")
        ve(lambda e: e.reduce_max(gt[0:1, 8:9], gt[0:1, 0:8], axis=AX))
        ve(lambda e: e.tensor_single_scalar(gt[0:1, 16:24], gt[0:1, 0:8],
                                            gt[0:1, 8:9], AL.is_ge))
        ve(lambda e: e.scalar_tensor_tensor(gt[0:1, 16:24], gt[0:1, 16:24],
                                            -1048576.0, idxbig[0:1, :],
                                            AL.mult, AL.add))
        ve(lambda e: e.tensor_reduce(gt[0:1, 9:10], gt[0:1, 16:24], AX, AL.min))
        t_sel = ve(lambda e: e.tensor_copy(seli[0:1, 0:1], gt[0:1, 9:10]))

        p.wait("gpsimd", "vector", t_sel)
        p.ccnt["lg"] += 16 * 4

        def dyn_loads(e):
            sem = p.sems["lg"]
            with e.register("selr") as r, e.register("offr") as r2:
                e.reg_load(r, seli[0:1, 0:1])
                e.reg_mul(r2, r, 1024 * 2048)
                e.dma_start(out=b_w1[:, :],
                            in_=bass.AP(f0_W1, r2,
                                        [[2048, 128], [128 * 2048, 8], [1, 2048]])
                            ).then_inc(sem, 16)
                e.reg_mul(r2, r, 2048 * 1024)
                e.dma_start(out=b_w2[:, :],
                            in_=bass.AP(f0_W2, r2,
                                        [[1024, 128], [128 * 1024, 16], [1, 1024]])
                            ).then_inc(sem, 16)
                e.reg_mul(r2, r, 128 * 16)
                e.dma_start(out=f0b1s[:, :],
                            in_=bass.AP(f0_b1, r2, [[16, 128], [1, 16]])
                            ).then_inc(sem, 16)
                e.reg_mul(r2, r, 128 * 8)
                e.dma_start(out=f0b2s[:, :],
                            in_=bass.AP(f0_b2, r2, [[8, 128], [1, 8]])
                            ).then_inc(sem, 16)
            return e.nop()
        p.emit("gpsimd", dyn_loads, inc=0)
        dma("gpsimd", "lg", b_wo[:, :],
            bass.AP(f0_Wo, 0, [[1024, 128], [128 * 1024, 8], [1, 1024]]))
        t_wsel = p.ccnt["lg"]
        p.waitc("tensor", "lg", t_wsel)
        p.waitc("scalar", "lg", t_wsel)

        for g in range(8):
            t_lng = layernorm_cg(
                G2, lambda s, g=g: xap(s, g, G2),
                lambda s: b_yg[:, s * G2:(s + 1) * G2],
                [(lambda s: b_ybf[:, s * G2:(s + 1) * G2], None, None),
                 (lambda s: b_yg[:, s * G2:(s + 1) * G2], None, None)],
                b_sq2, b_rsc, b_tmpln)
            p.wait("tensor", "vector", t_lng)
            for fc in range(16):
                bk = 2 + (fc % 2)
                mm_group(bk, [(b_w1[:, dc * 2048 + fc * 128:dc * 2048 + fc * 128 + 128],
                               b_ybf[:, dc * G2:(dc + 1) * G2]) for dc in range(NS)],
                         pb[bk][:, :G2])
                bank_read(bk, "scalar")
                se(lambda e, fc=fc, bk=bk: e.activation(
                    b_h1[:, fc * G2:(fc + 1) * G2], pb[bk][:, :G2], AF.Gelu,
                    bias=f0b1s[:, fc:fc + 1]))
                bank_rdone(bk, "scalar")
            t_h1p2 = p.tok("scalar")
            p.wait("tensor", "scalar", t_h1p2)
            for dc in range(NS):
                bk = 4 + (dc % 2)
                mm_group(bk, [(b_w2[:, fc * 1024 + dc * 128:fc * 1024 + dc * 128 + 128],
                               b_h1[:, fc * G2:(fc + 1) * G2]) for fc in range(16)],
                         pb[bk][:, :G2])
                bank_read(bk, "scalar")
                se(lambda e, dc=dc, bk=bk: e.activation(
                    b_rsn[:, dc * G2:(dc + 1) * G2], pb[bk][:, :G2], AF.Identity,
                    bias=f0b2s[:, dc:dc + 1]))
                bank_rdone(bk, "scalar")
            t_rsn = p.tok("scalar")
            p.wait("tensor", "scalar", t_rsn)
            for dc in range(NS):
                bk = 6 + (dc % 2)
                mm_group(bk, [(b_wo[:, kc * 1024 + dc * 128:kc * 1024 + dc * 128 + 128],
                               b_rsn[:, kc * G2:(kc + 1) * G2]) for kc in range(NS)],
                         pb[bk][:, :G2])
                bank_read(bk, "vector")
                ve(lambda e, dc=dc, bk=bk: e.scalar_tensor_tensor(
                    b_tmpln[:, :G2], pb[bk][:, :G2], 0.3,
                    b_yg[:, dc * G2:(dc + 1) * G2], AL.mult, AL.add))
                ve(lambda e, dc=dc, g=g: e.scalar_tensor_tensor(
                    xap(dc, g, G2), b_tmpln[:, :G2], f0bo3[:, dc:dc + 1],
                    xap(dc, g, G2), AL.add, AL.add))
                bank_rdone(bk, "vector")
        p.barrier()

    if phases >= 3:
        G3 = 512
        c_wqkv = sb("c_wqkv", (128, 24576), BF16, S0)
        c_wo1 = sb("c_wo1", (128, 8192), BF16, S0 + 49152)
        c_invt = sb("c_invt", (128, T), F32, S0 + 65536)
        c_ybf = sb("c_ybf", (128, 4096), BF16, S0 + 73728)
        c_qdm = sb("c_qdm", (128, 4096), BF16, S0 + 81920)
        c_kdm = sb("c_kdm", (128, 4096), BF16, S0 + 90112)
        c_xc = sb("c_xc", (128, 4096), F32, S0 + 81920)
        c_ktm = sb("c_ktm", (128, 4096), BF16, S0 + 98304)
        c_vtm = sb("c_vtm", (128, 4096), BF16, S0 + 106496)
        c_heads = sb("c_heads", (128, 4096), BF16, S0 + 73728)
        c_pref = sb("c_pref", (128, 512), F32, S0 + 114688)
        c_prbf = sb("c_prbf", (128, 64), BF16, S0 + 116736)
        c_sTb = sb("c_sTb", (128, 128), BF16, S0 + 116864)
        c_sq2 = sb("c_sq2", (128, 1024), F32, S0 + 116992)
        c_rsc = sb("c_rsc", (128, 512), F32, S0 + 121088)
        c_tmpln = c_rsc
        assert S0 + 123136 <= SB_END

        for dc in range(NS):
            dma("gpsimd", "lg", c_wqkv[:, dc * 3072:(dc + 1) * 3072],
                bass.AP(l1_wqkv, dc * 128 * 3072, [[3072, 128], [1, 3072]]))
        dma("gpsimd", "lg", c_wo1[:, :],
            bass.AP(l1_wo, 0, [[1024, 128], [128 * 1024, 8], [1, 1024]]))
        t_w3i = dma("sync", "ld", c_invt[:, :], c_invtot.ap())
        t_w3 = p.ccnt["lg"]
        p.waitc("tensor", "lg", t_w3)
        p.waitc("vector", "lg", t_w3)
        p.waitc("vector", "ld", t_w3i)
        tvz = ve(lambda e: e.memset(c_pref[:, :], 0.0))

        t_pref_use = [0, 0]
        t_sTb_use = [0, 0]
        for g in range(4):
            t_ln3 = layernorm_cg(
                G3, lambda s, g=g: xap(s, g, G3),
                lambda s: c_xc[:, s * G3:(s + 1) * G3],
                [(lambda s: c_ybf[:, s * G3:(s + 1) * G3], None, None)],
                c_sq2, c_rsc, c_tmpln)
            p.wait("tensor", "vector", t_ln3)
            for o_ in range(NS):
                bk = o_ % 2
                mm_group(bk, [(c_wqkv[:, dc * 3072 + o_ * 128:dc * 3072 + o_ * 128 + 128],
                               c_ybf[:, dc * G3:(dc + 1) * G3]) for dc in range(NS)],
                         pb[bk][:, :G3])
                bank_read(bk, "scalar")
                se(lambda e, o_=o_, bk=bk: e.activation(
                    c_qdm[:, o_ * G3:(o_ + 1) * G3], pb[bk][:, :G3], AF.Identity))
                bank_rdone(bk, "scalar")
            for o_ in range(NS):
                bk = o_ % 2
                mm_group(bk, [(c_wqkv[:, dc * 3072 + 1024 + o_ * 128:
                                      dc * 3072 + 1024 + o_ * 128 + 128],
                               c_ybf[:, dc * G3:(dc + 1) * G3]) for dc in range(NS)],
                         pb[bk][:, :G3])
                bank_read(bk, "vector")
                ve(lambda e, o_=o_, bk=bk: e.tensor_scalar(
                    c_kdm[:, o_ * G3:(o_ + 1) * G3], pb[bk][:, :G3], 0.0, 0.5,
                    AL.is_ge, AL.subtract))
                bank_rdone(bk, "vector")
            for (dst, col0) in ((c_ktm, 1024), (c_vtm, 2048)):
                for tt in range(4):
                    for hf in range(2):
                        bk = (tt * 2 + hf) % 2
                        mm_group(bk, [(c_ybf[:, dc * G3 + tt * 128:dc * G3 + tt * 128 + 128],
                                       c_wqkv[:, dc * 3072 + col0 + hf * 512:
                                              dc * 3072 + col0 + hf * 512 + 512])
                                      for dc in range(NS)], pb[bk][:, :512])
                        bank_read(bk, "vector")
                        ve(lambda e, dst=dst, tt=tt, hf=hf, bk=bk: e.tensor_scalar(
                            dst[:, tt * 1024 + hf * 512:tt * 1024 + hf * 512 + 512],
                            pb[bk][:, :512], 0.0, 0.5, AL.is_ge, AL.subtract))
                        bank_rdone(bk, "vector")
            t_qkv = p.tok("vector")
            p.wait("tensor", "vector", t_qkv)
            p.wait("vector", "vector", 0)
            if g == 0:
                p.wait("tensor", "vector", tvz)

            for nl in range(8):
                for h in range(16):
                    par = h % 2
                    pof = par * 64
                    so = (h // 2) * G3 + nl * 64
                    to = (nl // 2) * 1024 + h * 64
                    tp = (nl % 2) * 64
                    p.wait("vector", "tensor", t_pref_use[par])
                    tpb = ve(lambda e, pof=pof, h=h: e.tensor_copy(
                        c_prbf[pof:pof + 64, :],
                        c_pref[pof:pof + 64, (h // 2) * 64:(h // 2) * 64 + 64]))
                    bk = 2 + par
                    mm_group(bk, [(c_kdm[pof:pof + 64, so:so + 64],
                                   c_qdm[pof:pof + 64, so:so + 64])],
                             pb[bk][tp:tp + 64, 0:64])
                    bank_read(bk, "vector")
                    p.wait("vector", "tensor", t_sTb_use[par])
                    tsb = ve(lambda e, tp=tp, par=par, bk=bk: e.tensor_mul(
                        c_sTb[tp:tp + 64, par * 64:par * 64 + 64],
                        pb[bk][tp:tp + 64, 0:64], maskT[tp:tp + 64, :]))
                    bank_rdone(bk, "vector")
                    bk2 = 4 + par
                    bank_prep(bk2)
                    p.wait("tensor", "vector", tpb)
                    mm(c_prbf[pof:pof + 64, :], c_qdm[pof:pof + 64, so:so + 64],
                       pb[bk2][pof:pof + 64, 0:64], start=True, stop=False)
                    p.wait("tensor", "vector", tsb)
                    mm(c_vtm[tp:tp + 64, to:to + 64],
                       c_sTb[tp:tp + 64, par * 64:par * 64 + 64],
                       pb[bk2][pof:pof + 64, 0:64], start=False, stop=True, inc=1)
                    t_pref_use[par] = p.tok("tensor")
                    t_sTb_use[par] = p.tok("tensor")
                    bank_done(bk2)
                    bank_read(bk2, "vector")
                    ve(lambda e, pof=pof, so=so, g=g, nl=nl, bk2=bk2: e.tensor_mul(
                        c_heads[pof:pof + 64, so:so + 64],
                        pb[bk2][pof:pof + 64, 0:64],
                        c_invt[pof:pof + 64, g * G3 + nl * 64:g * G3 + nl * 64 + 64]))
                    bank_rdone(bk2, "vector")
                    bk3 = 6 + par
                    mm_group(bk3, [(c_ktm[tp:tp + 64, to:to + 64],
                                    c_vtm[tp:tp + 64, to:to + 64])],
                             pb[bk3][pof:pof + 64, 0:64])
                    bank_read(bk3, "vector")
                    ve(lambda e, pof=pof, h=h, bk3=bk3: e.tensor_add(
                        c_pref[pof:pof + 64, (h // 2) * 64:(h // 2) * 64 + 64],
                        c_pref[pof:pof + 64, (h // 2) * 64:(h // 2) * 64 + 64],
                        pb[bk3][pof:pof + 64, 0:64]))
                    bank_rdone(bk3, "vector")
            t_hd3 = p.tok("vector")
            p.wait("tensor", "vector", t_hd3)
            for dc in range(NS):
                bk = dc % 2
                mm_group(bk, [(c_wo1[:, kc * 1024 + dc * 128:kc * 1024 + dc * 128 + 128],
                               c_heads[:, kc * G3:(kc + 1) * G3]) for kc in range(NS)],
                         pb[bk][:, :G3])
                bank_read(bk, "vector")
                ve(lambda e, dc=dc, g=g, bk=bk: e.tensor_add(
                    xap(dc, g, G3), xap(dc, g, G3), pb[bk][:, :G3]))
                bank_rdone(bk, "vector")
        p.barrier()

    if phases >= 4:
        d_ybf = sb("d_ybf", (128, 16384), BF16, S0)
        d_h2 = sb("d_h2", (128, 32768), BF16, S0 + 32768)
        d_xc = sb("d_xc", (128, 16384), F32, S0 + 32768)
        d_ye = sb("d_ye", (128, 8192), BF16, S0 + 98304)
        d_wd = sb("d_wd", (128, 4096), BF16, S0 + 114688)
        d_wg = sb("d_wg", (128, 2048), BF16, S0 + 114688)
        d_wu = sb("d_wu", (128, 2048), BF16, S0 + 118784)
        d_sq2 = sb("d_sq2", (128, 1024), F32, S0 + 98304)
        d_rsc = sb("d_rsc", (128, 512), F32, S0 + 102400)
        d_tmpln = d_rsc
        assert S0 + 122880 <= SB_END

        for cg in range(4):
            layernorm_cg(
                512, lambda s, cg=cg: x[:, s * T + cg * 512:s * T + cg * 512 + 512],
                lambda s, cg=cg: d_xc[:, s * 2048 + cg * 512:s * 2048 + cg * 512 + 512],
                [(lambda s, cg=cg: d_ybf[:, s * 2048 + cg * 512:s * 2048 + cg * 512 + 512],
                  None, None)],
                d_sq2, d_rsc, d_tmpln)
        t_ln4 = p.tok("vector")
        p.wait("tensor", "vector", t_ln4)

        for tt in range(16):
            bk = tt % 2
            mm_group(bk, [(d_ybf[:, dc * 2048 + tt * 128:dc * 2048 + tt * 128 + 128],
                           gatebf[:, dc * 8:(dc + 1) * 8]) for dc in range(NS)],
                     pb[bk][:, 0:8])
            bank_read(bk, "vector")
            ve(lambda e, bk=bk: e.tensor_copy(gsc[:, 0:8], pb[bk][:, 0:8]))
            bank_rdone(bk, "vector")
            ve(lambda e: e.reduce_max(gsc[:, 32:33], gsc[:, 0:8], axis=AX))
            ve(lambda e: e.tensor_single_scalar(gsc[:, 8:16], gsc[:, 0:8],
                                                gsc[:, 32:33], AL.is_ge))
            ve(lambda e: e.scalar_tensor_tensor(gsc[:, 16:24], gsc[:, 8:16], -1e30,
                                                gsc[:, 0:8], AL.mult, AL.add))
            ve(lambda e: e.reduce_max(gsc[:, 33:34], gsc[:, 16:24], axis=AX))
            ve(lambda e: e.tensor_single_scalar(gsc[:, 24:32], gsc[:, 16:24],
                                                gsc[:, 33:34], AL.is_ge))
            tnv = ve(lambda e: e.tensor_scalar_mul(gsc[:, 34:35], gsc[:, 32:33], -1.0))
            p.wait("scalar", "vector", tnv)
            tp2 = se(lambda e: e.activation(gsc[:, 35:36], gsc[:, 33:34], AF.Sigmoid,
                                            bias=gsc[:, 34:35]))
            p.wait("vector", "scalar", tp2)
            ve(lambda e: e.tensor_scalar(gsc[:, 36:37], gsc[:, 35:36], -1.0, 1.0,
                                         AL.mult, AL.add))
            ve(lambda e: e.tensor_single_scalar(gsc[:, 40:48], gsc[:, 8:16],
                                                gsc[:, 36:37], AL.mult))
            tcb = ve(lambda e: e.scalar_tensor_tensor(gsc[:, 40:48], gsc[:, 24:32],
                                                      gsc[:, 35:36], gsc[:, 40:48],
                                                      AL.mult, AL.add))
            p.wait("tensor", "vector", tcb)
            bk2 = 2 + tt % 2
            bank_prep(bk2)
            p.emit("tensor", lambda e, bk2=bk2: e.transpose(
                pb[bk2][0:8, 0:128], gsc[:, 40:48], ident[:, :]), inc=1)
            bank_done(bk2)
            bank_read(bk2, "vector")
            ve(lambda e, tt=tt, bk2=bk2: e.tensor_copy(
                comb[:, tt * 128:(tt + 1) * 128], pb[bk2][0:8, 0:128]))
            bank_rdone(bk2, "vector")
        t_comb = p.tok("vector")
        p.wait("tensor", "vector", t_comb)

        iters = [(G, e_) for G in range(2) for e_ in range(8)]

        def build_ye(G, e_):
            for hf in range(2):
                bk = 6 + hf
                mm_group(bk, [(bcast8[:, e_ * 128:(e_ + 1) * 128],
                               comb[:, G * 1024 + hf * 512:G * 1024 + hf * 512 + 512])],
                         pb[bk][:, :512])
                bank_read(bk, "vector")
                for s in range(NS):
                    ve(lambda e, s=s, hf=hf, G=G, bk=bk: e.tensor_mul(
                        d_ye[:, s * 1024 + hf * 512:s * 1024 + hf * 512 + 512],
                        d_ybf[:, s * 2048 + G * 1024 + hf * 512:
                              s * 2048 + G * 1024 + hf * 512 + 512],
                        pb[bk][:, :512]))
                bank_rdone(bk, "vector")
            return p.tok("vector")

        def issue_wgwu(G, e_, fc, par):
            dma("gpsimd", "w%d" % par, d_wg[:, par * 1024:par * 1024 + 1024],
                bass.AP(f1_wg, e_ * D * 4096 + fc * 128,
                        [[4096, 128], [128 * 4096, 8], [1, 128]]))
            return dma("gpsimd", "w%d" % par,
                       d_wu[:, par * 1024:par * 1024 + 1024],
                       bass.AP(f1_wu, e_ * D * 4096 + fc * 128,
                               [[4096, 128], [128 * 4096, 8], [1, 128]]))

        t_ye = build_ye(*iters[0])
        t_wguse = [0] * 32
        t_wd_last = 0
        for it, (G, e_) in enumerate(iters):
            if it > 0:
                p.wait("gpsimd", "tensor", t_wd_last)
            t_wld = [issue_wgwu(G, e_, 0, 0), issue_wgwu(G, e_, 1, 1)]
            p.wait("tensor", "vector", t_ye)
            for fc in range(32):
                par = fc % 2
                p.waitc("tensor", "w%d" % par, t_wld[par])
                for hf in range(2):
                    bkg = hf
                    mm_group(bkg, [(d_wg[:, par * 1024 + dc * 128:par * 1024 + dc * 128 + 128],
                                    d_ybf[:, dc * 2048 + G * 1024 + hf * 512:
                                          dc * 2048 + G * 1024 + hf * 512 + 512])
                                   for dc in range(NS)], pb[bkg][:, :512])
                    bku = 2 + hf
                    mm_group(bku, [(d_wu[:, par * 1024 + dc * 128:par * 1024 + dc * 128 + 128],
                                    d_ye[:, dc * 1024 + hf * 512:
                                         dc * 1024 + hf * 512 + 512])
                                   for dc in range(NS)], pb[bku][:, :512])
                t_wguse[fc] = p.tok("tensor")
                nxt = fc + 2
                if nxt < 32:
                    p.wait("gpsimd", "tensor", t_wguse[fc])
                    t_wld[par] = issue_wgwu(G, e_, nxt, par)
                for hf in range(2):
                    bkg, bku = hf, 2 + hf
                    bank_read(bkg, "scalar")
                    tsl = se(lambda e, fc=fc, hf=hf, bkg=bkg: e.activation(
                        d_h2[:, fc * 1024 + hf * 512:fc * 1024 + hf * 512 + 512],
                        pb[bkg][:, :512], AF.Silu))
                    bank_rdone(bkg, "scalar")
                    p.wait("vector", "scalar", tsl)
                    bank_read(bku, "vector")
                    ve(lambda e, fc=fc, hf=hf, bku=bku: e.tensor_mul(
                        d_h2[:, fc * 1024 + hf * 512:fc * 1024 + hf * 512 + 512],
                        d_h2[:, fc * 1024 + hf * 512:fc * 1024 + hf * 512 + 512],
                        pb[bku][:, :512]))
                    bank_rdone(bku, "vector")
            t_h2 = p.tok("vector")
            p.wait("tensor", "vector", t_h2)
            p.wait("gpsimd", "tensor", t_wguse[31])
            t_wduse = {}
            for dc in range(NS):
                for half in range(2):
                    par = (dc * 2 + half) % 2
                    if dc * 2 + half >= 2:
                        p.wait("gpsimd", "tensor", t_wduse[dc * 2 + half - 2])
                    tld = dma("gpsimd", "u%d" % par,
                              d_wd[:, par * 2048:par * 2048 + 2048],
                              bass.AP(f1_wd,
                                      e_ * 4096 * D + half * 16 * 128 * 1024 + dc * 128,
                                      [[1024, 128], [128 * 1024, 16], [1, 128]]))
                    p.waitc("tensor", "u%d" % par, tld)
                    for hf in range(2):
                        bk = 4 + hf
                        bank_prep(bk) if (half == 0) else None
                        for j, fc in enumerate(range(half * 16, half * 16 + 16)):
                            mm(d_wd[:, par * 2048 + j * 128:par * 2048 + j * 128 + 128],
                               d_h2[:, fc * 1024 + hf * 512:fc * 1024 + hf * 512 + 512],
                               pb[bk][:, :512], start=(fc == 0), stop=(fc == 31),
                               inc=(1 if j == 15 else 0))
                        if half == 1:
                            bank_done(bk)
                    t_wduse[dc * 2 + half] = p.tok("tensor")
                    t_wd_last = p.tok("tensor")
                for hf in range(2):
                    bk = 4 + hf
                    bank_read(bk, "vector")
                    ve(lambda e, dc=dc, hf=hf, G=G, bk=bk: e.tensor_add(
                        x[:, dc * T + G * 1024 + hf * 512:
                          dc * T + G * 1024 + hf * 512 + 512],
                        x[:, dc * T + G * 1024 + hf * 512:
                          dc * T + G * 1024 + hf * 512 + 512],
                        pb[bk][:, :512]))
                    bank_rdone(bk, "vector")
                if dc == 0 and it + 1 < len(iters):
                    t_ye = build_ye(*iters[it + 1])
        p.barrier()

    p.barrier(engs=("sync",))
    tout = dma("sync", "out", bass.AP(out_d, 0, [[T, 128], [128 * T, NS], [1, T]]),
               x[:, :])
    p.emit("sync", lambda e, t=tout: e.wait_ge(p.sems["out"], t))

    p.finalize()
    return nc


LN_EPS_H = 1e-5
PI_2_H = math.pi / 2


def prep_inputs(inputs):
    f = np.float32
    c = np.ascontiguousarray
    WQ = inputs["L0_attn_W_Q"]; WK = inputs["L0_attn_W_K"]
    WV = inputs["L0_attn_W_V"]; WO = inputs["L0_attn_W_O"]
    base = {
        "l0_wq16": c(WQ[:, :, 0].T.astype(f)),
        "l0_wk16": c(WK[:, :, 0].T.astype(f)),
        "l0_wv": c(WV.transpose(1, 0, 2).reshape(1024, 1024).astype(f)),
        "l0_wo": c(WO.transpose(2, 1, 0).reshape(1024, 1024).astype(f)),
        "l0_ln1w": c(inputs["L0_attn_ln1_w"].reshape(8, 128).T.astype(f)),
        "l0_ln1b": c(inputs["L0_attn_ln1_b"].reshape(8, 128).T.astype(f)),
        "l0_ln2w": c(inputs["L0_attn_ln2_w"].reshape(8, 128).T.astype(f)),
        "l0_ln2b": c(inputs["L0_attn_ln2_b"].reshape(8, 128).T.astype(f)),
        "l0_W1": c(inputs["L0_attn_W1"].astype(f)),
        "l0_b1": c(inputs["L0_attn_b1"].reshape(32, 128).T.astype(f)),
        "l0_W2": c(inputs["L0_attn_W2"].astype(f)),
        "l0_b2": c(inputs["L0_attn_b2"].reshape(8, 128).T.astype(f)),
        "f0_Wp1": c(inputs["L0_ffn_Wp1"].astype(f)),
        "f0_bp1": c(inputs["L0_ffn_bp1"].reshape(128, 1).astype(f)),
        "f0_Wp2": c(inputs["L0_ffn_Wp2"].astype(f)),
        "f0_bp2": c(inputs["L0_ffn_bp2"].reshape(1, 8).astype(f)),
        "f0_W1": c(inputs["L0_ffn_W1"].astype(f)),
        "f0_b1": c(inputs["L0_ffn_b1"].reshape(8, 16, 128).transpose(0, 2, 1).astype(f)),
        "f0_W2": c(inputs["L0_ffn_W2"].astype(f)),
        "f0_b2": c(inputs["L0_ffn_b2"].reshape(8, 8, 128).transpose(0, 2, 1).astype(f)),
        "f0_Wo": c(inputs["L0_ffn_Wo"].astype(f)),
        "f0_bo": c(inputs["L0_ffn_bo"].reshape(8, 128).T.astype(f)),
        "l1_wqkv": c(inputs["L1_attn_W_qkv"].T.astype(f)),
        "l1_wo": c(inputs["L1_attn_W_o"].T.astype(f)),
        "f1_gate": c(inputs["L1_ffn_gate_w"].T.astype(f)),
        "f1_wg": c(inputs["L1_ffn_wg"].transpose(0, 2, 1).astype(f)),
        "f1_wu": c(inputs["L1_ffn_wu"].transpose(0, 2, 1).astype(f)),
        "f1_wd": c(inputs["L1_ffn_wd"].transpose(0, 2, 1).astype(f)),
        "c_ones_bf": np.ones((128, 128), dtype=ml_dtypes.bfloat16),
        "c_ones_f": np.ones((128, 128), dtype=f),
        "c_ident": np.eye(128, dtype=f),
        "c_idxbig": (np.arange(8, dtype=f) + 1048576.0).reshape(1, 8),
        "c_flt": c(np.broadcast_to(
            np.array([LN_EPS_H, PI_2_H], dtype=f), (128, 2)).astype(f)),
    }
    sel = np.zeros((16, 1024), dtype=f)
    for k in range(16):
        s, r = divmod(k, 2)
        sel[k, s * 128 + r * 64:s * 128 + r * 64 + 64] = 1.0
    base["c_sel16"] = sel
    bc8 = np.zeros((8, 1024), dtype=f)
    for e in range(8):
        bc8[e, e * 128:(e + 1) * 128] = 1.0
    base["c_bcast8"] = bc8
    mT = np.triu(np.ones((64, 64), dtype=f))
    base["c_maskT"] = c(np.concatenate([mT, mT], axis=0))
    base["c_invtot"] = c(np.broadcast_to(
        0.5 / (np.arange(T, dtype=np.float64) + 1.0), (128, T)).astype(f))
    xs = inputs["L0_attn_x"]
    maps = []
    for b in range(8):
        m = dict(base)
        m["x_in"] = c(xs[b].T.astype(f))
        maps.append(m)
    return maps


def kernel(**inputs):
    maps = prep_inputs(inputs)
    nc = build_core_program(4)
    res = run_bass_kernel_spmd(nc, maps, core_ids=list(range(8)))
    out = np.stack([np.asarray(r["out"]).T for r in res.results], axis=0)
    return out.astype(np.float32)


# version 04 (final)
# speedup vs baseline: 1.0510x; 1.0132x over previous
import math
import contextlib
import numpy as np
import ml_dtypes
import concourse.bass as bass
import concourse.mybir as mybir
from concourse.bass_utils import run_bass_kernel_spmd

F32 = mybir.dt.float32
BF16 = mybir.dt.bfloat16
I32 = mybir.dt.int32
AF = mybir.ActivationFunctionType
AL = mybir.AluOpType
AX = mybir.AxisListType.X

D = 1024
T = 2048
NS = 8
LN_EPS = 1e-5
COS_EPS = 1e-6
PI = math.pi

ENGS = ("tensor", "vector", "scalar", "gpsimd", "sync")
CHANNELS = ("ld", "lg", "w0", "w1", "u0", "u1", "out")


class Prog:

    def __init__(self, nc):
        self.nc = nc
        self.ops = {e: [] for e in ENGS}
        self.cnt = {e: 0 for e in ENGS}
        self.ccnt = {c: 0 for c in CHANNELS}
        self.sems = {}

    def emit(self, eng, fn, inc=0):
        self.ops[eng].append((fn, inc, eng in ("vector", "scalar")))
        if inc:
            self.cnt[eng] += inc
        return self.cnt[eng]

    def wait(self, eng, prod, tok):
        if tok <= 0 or prod == eng:
            return
        self.ops[eng].append(
            (lambda e, prod=prod, tok=tok: e.wait_ge(self.sems[prod], tok), 0, False))

    def waitc(self, eng, ch, tok):
        if tok <= 0:
            return
        self.ops[eng].append(
            (lambda e, ch=ch, tok=tok: e.wait_ge(self.sems[ch], tok), 0, False))

    def tok(self, eng):
        return self.cnt[eng]

    def barrier(self, engs=("tensor", "vector", "scalar", "gpsimd", "sync")):
        toks = {e: self.cnt[e] for e in ENGS}
        ctoks = {c: self.ccnt[c] for c in CHANNELS}
        for e in engs:
            for prod in ENGS:
                if prod != e and toks[prod] > 0:
                    self.wait(e, prod, toks[prod])
            for c in CHANNELS:
                if ctoks[c] > 0:
                    self.waitc(e, c, ctoks[c])

    def finalize(self):
        nc = self.nc
        with contextlib.ExitStack() as st:
            for e in ENGS:
                self.sems[e] = st.enter_context(nc.semaphore(f"sem_{e}"))
            for c in CHANNELS:
                self.sems[c] = st.enter_context(nc.semaphore(f"ch_{c}"))
            with nc.Block() as block:
                for e in ENGS:
                    def run(proxy, e=e):
                        sem = self.sems[e]
                        for fn, inc, dr in self.ops[e]:
                            inst = fn(proxy)
                            if inc:
                                inst.then_inc(sem, inc)
                            if dr:
                                proxy.drain()
                    getattr(block, e)(run)


def build_core_program(phases=4):
    nc = bass.Bass("TRN2", target_bir_lowering=False)

    def din(name, shape, dtype=F32):
        return nc.dram_tensor(name, list(shape), dtype, kind="ExternalInput")

    x_in = din("x_in", (D, T))
    l0_wq16 = din("l0_wq16", (D, 16))
    l0_wk16 = din("l0_wk16", (D, 16))
    l0_wv = din("l0_wv", (D, D))
    l0_wo = din("l0_wo", (D, D))
    l0_ln1w = din("l0_ln1w", (128, 8)); l0_ln1b = din("l0_ln1b", (128, 8))
    l0_ln2w = din("l0_ln2w", (128, 8)); l0_ln2b = din("l0_ln2b", (128, 8))
    l0_W1 = din("l0_W1", (D, 4096)); l0_b1 = din("l0_b1", (128, 32))
    l0_W2 = din("l0_W2", (4096, D)); l0_b2 = din("l0_b2", (128, 8))
    f0_Wp1 = din("f0_Wp1", (D, 128)); f0_bp1 = din("f0_bp1", (128, 1))
    f0_Wp2 = din("f0_Wp2", (128, 8)); f0_bp2 = din("f0_bp2", (1, 8))
    f0_W1 = din("f0_W1", (8, D, 2048)); f0_b1 = din("f0_b1", (8, 128, 16))
    f0_W2 = din("f0_W2", (8, 2048, D)); f0_b2 = din("f0_b2", (8, 128, 8))
    f0_Wo = din("f0_Wo", (D, D)); f0_bo = din("f0_bo", (128, 8))
    l1_wqkv = din("l1_wqkv", (D, 3072))
    l1_wo = din("l1_wo", (D, D))
    f1_gate = din("f1_gate", (D, 8))
    f1_wg = din("f1_wg", (8, D, 4096))
    f1_wu = din("f1_wu", (8, D, 4096))
    f1_wd = din("f1_wd", (8, 4096, D))
    c_ones_bf = din("c_ones_bf", (128, 128), BF16)
    c_ones_f = din("c_ones_f", (128, 128))
    c_ident = din("c_ident", (128, 128))
    c_sel16 = din("c_sel16", (16, 1024))
    c_bcast8 = din("c_bcast8", (8, 1024))
    c_maskT = din("c_maskT", (128, 64))
    c_invtot = din("c_invtot", (128, T))
    c_idxbig = din("c_idxbig", (1, 8))
    c_flt = din("c_flt", (128, 2))
    out_d = nc.dram_tensor("out", [D, T], F32, kind="ExternalOutput")

    SB_BASE = 16512

    def sb(name, shape, dtype, off):
        assert off % 32 == 0, (name, off)
        return nc.alloc_sbuf_tensor_at(name, list(shape), dtype,
                                       offset=off + SB_BASE)

    SB_END = 212832
    x = sb("x", (128, NS * T), F32, 0)
    o = 65536
    def psb(name, shape, dtype, nbytes):
        nonlocal o
        t = sb(name, shape, dtype, o)
        o += nbytes
        return t
    ones_bf = psb("ones_bf", (128, 128), BF16, 256)
    ones_f = psb("ones_f", (128, 128), F32, 512)
    ident = psb("ident", (128, 128), F32, 512)
    ln1w = psb("ln1w", (128, 8), F32, 32)
    ln1b = psb("ln1b", (128, 8), F32, 32)
    ln2w = psb("ln2w", (128, 8), F32, 32)
    ln2b = psb("ln2b", (128, 8), F32, 32)
    b1sb = psb("b1sb", (128, 32), F32, 128)
    b2sb = psb("b2sb", (128, 8), F32, 32)
    wp1sb = psb("wp1sb", (128, 1024), F32, 4096)
    bp1sb = psb("bp1sb", (128, 1), F32, 32)
    wp2sb = psb("wp2sb", (128, 8), F32, 32)
    bp2sb = psb("bp2sb", (1, 8), F32, 32)
    f0b1s = psb("f0b1s", (128, 16), F32, 64)
    f0b2s = psb("f0b2s", (128, 8), F32, 32)
    f0bos = psb("f0bos", (128, 8), F32, 32)
    f0bo3 = psb("f0bo3", (128, 8), F32, 32)
    sel16 = psb("sel16", (16, 1024), F32, 4096)
    bcast8 = psb("bcast8", (8, 1024), F32, 4096)
    maskT = psb("maskT", (128, 64), F32, 256)
    idxbig = psb("idxbig", (1, 8), F32, 32)
    gatebf = psb("gatebf", (128, 64), BF16, 128)
    comb = psb("comb", (8, T), F32, 8192)
    carry_ck = psb("carry_ck", (128, 8), F32, 32)
    carry_sk = psb("carry_sk", (128, 8), F32, 32)
    carry16 = psb("carry16", (16, 4), F32, 32)
    gsc = psb("gsc", (128, 64), F32, 256)
    gt = psb("gt", (1, 32), F32, 128)
    seli = psb("seli", (1, 2), I32, 32)
    cflt = psb("cflt", (128, 2), F32, 32)
    nc.const_aps.register(F32, float(LN_EPS), cflt[:, 0:1])
    nc.const_aps.register(F32, float(PI / 2), cflt[:, 1:2])
    assert o <= 89504, o
    S0 = 89504

    pb = [nc.alloc_psum_tensor(f"pb{i}", [128, 512], F32) for i in range(8)]

    p = Prog(nc)

    bank_w = [0] * 8
    bank_r = [[] for _ in range(8)]

    def bank_prep(i):
        for (eng, tok) in bank_r[i]:
            p.wait("tensor", eng, tok)
        bank_r[i] = []

    def bank_done(i):
        bank_w[i] = p.tok("tensor")

    def bank_read(i, eng):
        p.wait(eng, "tensor", bank_w[i])

    def bank_rdone(i, eng):
        bank_r[i].append((eng, p.tok(eng)))

    def dma(queue, ch, out_ap, in_ap):
        def f(e, a=out_ap, b=in_ap, ch=ch):
            return e.dma_start(out=a, in_=b).then_inc(p.sems[ch], 16)
        p.emit(queue, f, inc=0)
        p.ccnt[ch] += 16
        return p.ccnt[ch]

    def mm(lhsT, rhs, out, start, stop, inc=0):
        return p.emit("tensor",
                      lambda e, a=lhsT, b=rhs, c=out, s1=start, s2=stop:
                      e.matmul(c, a, b, start=s1, stop=s2), inc=inc)

    def ve(fn, inc=1):
        return p.emit("vector", fn, inc=inc)

    def se(fn, inc=1):
        return p.emit("scalar", fn, inc=inc)

    def mm_group(bank_i, lhsT_rhs_list, out):
        bank_prep(bank_i)
        n = len(lhsT_rhs_list)
        for i, (a, b) in enumerate(lhsT_rhs_list):
            mm(a, b, out, start=(i == 0), stop=(i == n - 1),
               inc=(1 if i == n - 1 else 0))
        bank_done(bank_i)

    def layernorm_cg(Wc, src, xc, outs, sq2, rsc, tmpln, src_tok=None):
        stok = p.tok("vector") if src_tok is None else src_tok
        xbf2 = sq2
        p.wait("scalar", "vector", stok)
        mm_tok = [0] * NS
        bank_prep(0)
        for s in range(NS):
            if s >= 2:
                p.wait("scalar", "tensor", mm_tok[s - 2])
            buf = xbf2[:, (s % 2) * Wc:(s % 2) * Wc + Wc]
            tv = se(lambda e, a=buf, b=src(s): e.activation(a, b, AF.Identity))
            p.wait("tensor", "scalar", tv)
            mm_tok[s] = mm(ones_bf[:, :], buf, pb[0][:, :Wc],
                           start=(s == 0), stop=(s == NS - 1), inc=1)
        bank_done(0)
        bank_read(0, "vector")
        p.wait("vector", "vector", 0)
        for s in range(NS):
            ve(lambda e, a=xc(s), b=src(s):
               e.scalar_tensor_tensor(a, pb[0][:, :Wc], -1.0 / 1024.0, b,
                                      AL.mult, AL.add))
        xc_tok = p.tok("vector")
        bank_rdone(0, "vector")
        mm2 = [0] * NS
        bank_prep(1)
        p.wait("scalar", "vector", xc_tok)
        for s in range(NS):
            if s >= 2:
                p.wait("scalar", "tensor", mm2[s - 2])
            buf = sq2[:, (s % 2) * Wc:(s % 2) * Wc + Wc]
            ts = se(lambda e, a=buf, b=xc(s): e.activation(a, b, AF.Square))
            p.wait("tensor", "scalar", ts)
            mm2[s] = mm(ones_bf[:, :], buf, pb[1][:, :Wc],
                        start=(s == 0), stop=(s == NS - 1), inc=1)
        bank_done(1)
        bank_read(1, "scalar")
        tsq = se(lambda e: e.activation(rsc[:, :Wc], pb[1][:, :Wc], AF.Sqrt,
                                        bias=float(LN_EPS), scale=1.0 / 1024.0))
        bank_rdone(1, "scalar")
        p.wait("vector", "scalar", tsq)
        ve(lambda e: e.reciprocal(rsc[:, :Wc], rsc[:, :Wc]))
        for s in range(NS):
            for (out_ap, wcol, bcol) in outs:
                if wcol is None:
                    ve(lambda e, a=out_ap(s), b=xc(s):
                       e.tensor_mul(a, b, rsc[:, :Wc]))
                else:
                    ve(lambda e, b=xc(s): e.tensor_mul(tmpln[:, :Wc], b, rsc[:, :Wc]))
                    ve(lambda e, a=out_ap(s), w=wcol(s), bb=bcol(s):
                       e.tensor_scalar(a, tmpln[:, :Wc], w, bb, AL.mult, AL.add))
        return p.tok("vector")

    def xap(s, g, W):
        return x[:, s * T + g * W: s * T + g * W + W]

    dma("sync", "ld", x[:, :], bass.AP(x_in, 0, [[T, 128], [128 * T, NS], [1, T]]))
    for (sbuf, dram) in [(ones_bf, c_ones_bf), (ones_f, c_ones_f),
                         (ident, c_ident), (ln1w, l0_ln1w), (ln1b, l0_ln1b),
                         (ln2w, l0_ln2w), (ln2b, l0_ln2b), (b1sb, l0_b1),
                         (b2sb, l0_b2), (bp1sb, f0_bp1), (wp2sb, f0_Wp2),
                         (bp2sb, f0_bp2), (f0bos, f0_bo), (sel16, c_sel16),
                         (bcast8, c_bcast8), (maskT, c_maskT),
                         (idxbig, c_idxbig), (cflt, c_flt)]:
        dma("sync", "ld", sbuf[:, :], dram.ap())
    dma("sync", "ld", wp1sb[:, :],
        bass.AP(f0_Wp1, 0, [[128, 128], [128 * 128, 8], [1, 128]]))
    dma("gpsimd", "lg", gatebf[:, :],
        bass.AP(f1_gate, 0, [[8, 128], [8 * 128, 8], [1, 8]]))

    if phases >= 1:
        G1 = 512
        a_y0 = sb("a_y0", (128, 8 * G1), F32, S0)
        a_y1 = sb("a_y1", (128, 8 * G1), BF16, S0 + 16384)
        a_wkrep = sb("a_wkrep", (128, 8192), BF16, S0 + 24576)
        a_wv = sb("a_wv", (128, 8192), BF16, S0 + 40960)
        a_h1 = sb("a_h1", (128, 16384), BF16, S0 + 24576)
        a_wo = sb("a_wo", (128, 8192), BF16, S0 + 57344)
        a_ckv = sb("a_ckv", (128, 8 * G1), F32, S0 + 73728)
        a_xc2 = sb("a_xc2", (128, 8 * G1), F32, S0 + 73728)
        a_skv = sb("a_skv", (128, 8 * G1), F32, S0 + 90112)
        a_w1c = sb("a_w1c", (128, 2048), BF16, S0 + 90112)
        a_w2c = sb("a_w2c", (128, 2048), BF16, S0 + 94208)
        a_heads = sb("a_heads", (128, 8 * G1), BF16, S0 + 16384)
        a_q16 = sb("a_q16", (16, 2560), F32, S0 + 106496)
        a_sq2 = sb("a_sq2", (128, 1024), BF16, S0 + 106496)
        a_tk = sb("a_tk", (128, 512), F32, S0 + 116736)
        a_tmpln = sb("a_tmpln", (128, 512), F32, S0 + 116736)
        a_tmp1 = sb("a_tmp1", (128, 512), F32, S0 + 118784)
        a_rsc = sb("a_rsc", (128, 512), F32, S0 + 118784)
        a_wq16 = sb("a_wq16", (128, 256), BF16, S0 + 120832)
        assert S0 + 121344 <= SB_END

        dma("gpsimd", "lg", a_wq16[:, 0:128],
            bass.AP(l0_wq16, 0, [[16, 128], [16 * 128, 8], [1, 16]]))
        dma("gpsimd", "lg", a_wq16[:, 128:256],
            bass.AP(l0_wk16, 0, [[16, 128], [16 * 128, 8], [1, 16]]))
        dma("gpsimd", "lg", a_wo[:, :],
            bass.AP(l0_wo, 0, [[1024, 128], [128 * 1024, 8], [1, 1024]]))
        dma("gpsimd", "lg", a_wv[:, :],
            bass.AP(l0_wv, 0, [[1024, 128], [128 * 1024, 8], [1, 1024]]))
        t_ld0 = p.ccnt["ld"]
        t_lg0 = p.ccnt["lg"]
        for e in ("vector", "scalar", "tensor", "gpsimd"):
            p.waitc(e, "ld", t_ld0)
            p.waitc(e, "lg", t_lg0)
        ve(lambda e: e.tensor_scalar_mul(f0bo3[:, :], f0bos[:, :], 0.3))
        ve(lambda e: e.memset(carry_ck[:, :], 0.0))
        ve(lambda e: e.memset(carry_sk[:, :], 0.0))
        ve(lambda e: e.memset(carry16[:, :], 0.0))

        def q16s(i):
            return a_q16[:, i * 512:(i + 1) * 512]

        for g in range(4):
            if g > 0:
                p.barrier(engs=("gpsimd",))
                dma("gpsimd", "lg", a_wv[:, :],
                    bass.AP(l0_wv, 0, [[1024, 128], [128 * 1024, 8], [1, 1024]]))
            twv = p.ccnt["lg"]
            p.waitc("vector", "lg", twv)
            for dc in range(NS):
                ve(lambda e, dc=dc: e.tensor_copy(
                    a_wkrep[:, dc * 1024:(dc + 1) * 1024],
                    bass.AP(a_wq16, 128 + dc * 16, [[256, 128], [1, 16], [0, 64]])))
            twk = p.tok("vector")
            p.wait("tensor", "vector", twk)
            p.waitc("tensor", "lg", twv)

            layernorm_cg(G1, lambda s, g=g: xap(s, g, G1),
                         lambda s: a_y0[:, s * G1:(s + 1) * G1],
                         [(lambda s: a_y0[:, s * G1:(s + 1) * G1], None, None)],
                         a_sq2, a_rsc, a_tmpln, src_tok=0)
            t_ln1 = layernorm_cg(
                G1, lambda s: a_y0[:, s * G1:(s + 1) * G1],
                lambda s: a_xc2[:, s * G1:(s + 1) * G1],
                [(lambda s: a_y1[:, s * G1:(s + 1) * G1],
                  lambda s: ln1w[:, s:s + 1], lambda s: ln1b[:, s:s + 1])],
                a_sq2, a_rsc, a_tmpln)
            p.wait("tensor", "vector", t_ln1)

            mm_group(2, [(a_wq16[:, dc * 16:dc * 16 + 16],
                          a_y1[:, dc * G1:(dc + 1) * G1]) for dc in range(NS)],
                     pb[2][0:16, :G1])
            mm_group(3, [(a_wq16[:, 128 + dc * 16:128 + dc * 16 + 16],
                          a_y1[:, dc * G1:(dc + 1) * G1]) for dc in range(NS)],
                     pb[3][0:16, :G1])
            bank_read(2, "scalar"); bank_read(3, "scalar")
            se(lambda e: e.activation(q16s(0)[0:16, :], pb[2][0:16, :G1], AF.Tanh))
            se(lambda e: e.activation(q16s(1)[0:16, :], q16s(0)[0:16, :], AF.Sin,
                                      bias=PI / 2, scale=PI / 4))
            se(lambda e: e.activation(q16s(2)[0:16, :], q16s(0)[0:16, :], AF.Sin,
                                      scale=PI / 4))
            se(lambda e: e.activation(q16s(0)[0:16, :], pb[3][0:16, :G1], AF.Tanh))
            se(lambda e: e.activation(q16s(3)[0:16, :], q16s(0)[0:16, :], AF.Sin,
                                      bias=PI / 2, scale=PI / 4))
            t_k16 = se(lambda e: e.activation(q16s(4)[0:16, :], q16s(0)[0:16, :],
                                              AF.Sin, scale=PI / 4))
            bank_rdone(2, "scalar"); bank_rdone(3, "scalar")
            p.wait("vector", "scalar", t_k16)
            for (i, cc) in ((3, 0), (4, 1)):
                ve(lambda e, i=i, cc=cc: e.tensor_tensor_scan(
                    q16s(i)[0:16, :], q16s(i)[0:16, :], q16s(i)[0:16, :],
                    carry16[0:16, cc:cc + 1], AL.add, AL.bypass))
                ve(lambda e, i=i, cc=cc: e.tensor_copy(
                    carry16[0:16, cc:cc + 1], a_q16[0:16, i * 512 + 511:i * 512 + 512]))
            ve(lambda e: e.tensor_mul(q16s(3)[0:16, :], q16s(1)[0:16, :], q16s(3)[0:16, :]))
            ve(lambda e: e.tensor_mul(q16s(4)[0:16, :], q16s(2)[0:16, :], q16s(4)[0:16, :]))
            ve(lambda e: e.tensor_add(q16s(3)[0:16, :], q16s(3)[0:16, :], q16s(4)[0:16, :]))
            ve(lambda e: e.tensor_scalar_add(q16s(3)[0:16, :], q16s(3)[0:16, :], COS_EPS))
            ve(lambda e: e.reciprocal(q16s(3)[0:16, :], q16s(3)[0:16, :]))
            ve(lambda e: e.tensor_mul(q16s(1)[0:16, :], q16s(1)[0:16, :], q16s(3)[0:16, :]))
            t_qd = ve(lambda e: e.tensor_mul(q16s(2)[0:16, :], q16s(2)[0:16, :],
                                             q16s(3)[0:16, :]))

            for s in range(NS):
                bk = 4 + (s % 2)
                mm_group(bk, [(a_wkrep[:, dc * 1024 + s * 128:dc * 1024 + s * 128 + 128],
                               a_y1[:, dc * G1:(dc + 1) * G1]) for dc in range(NS)],
                         pb[bk][:, :G1])
                bv_ = 6 + (s % 2)
                mm_group(bv_, [(a_wv[:, dc * 1024 + s * 128:dc * 1024 + s * 128 + 128],
                                a_y1[:, dc * G1:(dc + 1) * G1]) for dc in range(NS)],
                         pb[bv_][:, :G1])
                bank_read(bk, "scalar")
                se(lambda e, bk=bk: e.activation(a_tk[:, :], pb[bk][:, :G1], AF.Tanh))
                se(lambda e, s=s: e.activation(a_ckv[:, s * G1:(s + 1) * G1],
                                               a_tk[:, :], AF.Sin, bias=PI / 2,
                                               scale=PI / 4))
                tt = se(lambda e, s=s: e.activation(a_skv[:, s * G1:(s + 1) * G1],
                                                    a_tk[:, :], AF.Sin, scale=PI / 4))
                bank_rdone(bk, "scalar")
                p.wait("vector", "scalar", tt)
                bank_read(bv_, "vector")
                ve(lambda e, s=s, bv_=bv_: e.tensor_mul(
                    a_ckv[:, s * G1:(s + 1) * G1], a_ckv[:, s * G1:(s + 1) * G1],
                    pb[bv_][:, :G1]))
                ve(lambda e, s=s, bv_=bv_: e.tensor_mul(
                    a_skv[:, s * G1:(s + 1) * G1], a_skv[:, s * G1:(s + 1) * G1],
                    pb[bv_][:, :G1]))
                bank_rdone(bv_, "vector")
                for (buf, car) in ((a_ckv, carry_ck), (a_skv, carry_sk)):
                    ve(lambda e, s=s, buf=buf, car=car: e.tensor_tensor_scan(
                        buf[:, s * G1:(s + 1) * G1], buf[:, s * G1:(s + 1) * G1],
                        buf[:, s * G1:(s + 1) * G1], car[:, s:s + 1], AL.add, AL.bypass))
                    ve(lambda e, s=s, buf=buf, car=car: e.tensor_copy(
                        car[:, s:s + 1], buf[:, s * G1 + G1 - 1:s * G1 + G1]))

            p.wait("tensor", "vector", t_qd)
            for s in range(NS):
                bc, bs_ = (2, 3) if s % 2 == 0 else (4, 5)
                mm_group(bc, [(sel16[:, s * 128:(s + 1) * 128], q16s(1)[0:16, :])],
                         pb[bc][:, :G1])
                mm_group(bs_, [(sel16[:, s * 128:(s + 1) * 128], q16s(2)[0:16, :])],
                         pb[bs_][:, :G1])
                bank_read(bc, "vector"); bank_read(bs_, "vector")
                tsl = a_tmp1 if s % 2 == 0 else a_tk
                ve(lambda e, s=s, bc=bc: e.tensor_mul(
                    a_heads[:, s * G1:(s + 1) * G1],
                    a_ckv[:, s * G1:(s + 1) * G1], pb[bc][:, :G1]))
                ve(lambda e, s=s, bs_=bs_, tsl=tsl: e.tensor_mul(
                    tsl[:, :], a_skv[:, s * G1:(s + 1) * G1], pb[bs_][:, :G1]))
                ve(lambda e, s=s, tsl=tsl: e.tensor_add(
                    a_heads[:, s * G1:(s + 1) * G1],
                    a_heads[:, s * G1:(s + 1) * G1], tsl[:, :]))
                bank_rdone(bc, "vector"); bank_rdone(bs_, "vector")
            t_heads = p.tok("vector")
            p.wait("tensor", "vector", t_heads)

            for dc in range(NS):
                bk = dc % 2
                mm_group(bk, [(a_wo[:, kc * 1024 + dc * 128:kc * 1024 + dc * 128 + 128],
                               a_heads[:, kc * G1:(kc + 1) * G1]) for kc in range(NS)],
                         pb[bk][:, :G1])
                bank_read(bk, "vector")
                ve(lambda e, dc=dc, bk=bk: e.tensor_add(
                    a_y0[:, dc * G1:(dc + 1) * G1], a_y0[:, dc * G1:(dc + 1) * G1],
                    pb[bk][:, :G1]))
                ve(lambda e, dc=dc, g=g: e.tensor_add(
                    xap(dc, g, G1), xap(dc, g, G1), a_y0[:, dc * G1:(dc + 1) * G1]))
                bank_rdone(bk, "vector")

            t_ln2 = layernorm_cg(
                G1, lambda s: a_y0[:, s * G1:(s + 1) * G1],
                lambda s: a_y0[:, s * G1:(s + 1) * G1],
                [(lambda s: a_y1[:, s * G1:(s + 1) * G1],
                  lambda s: ln2w[:, s:s + 1], lambda s: ln2b[:, s:s + 1])],
                a_sq2, a_rsc, a_tmpln)
            p.wait("tensor", "vector", t_ln2)
            p.barrier(engs=("gpsimd",))

            t_w1use = [0] * 32
            for fc in range(32):
                half = fc % 2
                if fc >= 2:
                    p.wait("gpsimd", "tensor", t_w1use[fc - 2])
                tld = dma("gpsimd", "w%d" % half, a_w1c[:, half * 1024:half * 1024 + 1024],
                          bass.AP(l0_W1, fc * 128, [[4096, 128], [128 * 4096, 8], [1, 128]]))
                p.waitc("tensor", "w%d" % half, tld)
                bk = 2 + half
                mm_group(bk, [(a_w1c[:, half * 1024 + dc * 128:half * 1024 + dc * 128 + 128],
                               a_y1[:, dc * G1:(dc + 1) * G1]) for dc in range(NS)],
                         pb[bk][:, :G1])
                t_w1use[fc] = p.tok("tensor")
                bank_read(bk, "scalar")
                se(lambda e, fc=fc, bk=bk: e.activation(
                    a_h1[:, fc * G1:(fc + 1) * G1], pb[bk][:, :G1],
                    AF.Gelu_apprx_tanh, bias=b1sb[:, fc:fc + 1]))
                bank_rdone(bk, "scalar")
            t_h1 = p.tok("scalar")
            p.wait("tensor", "scalar", t_h1)

            for i in range(NS):
                bank_prep(i)
            t_w2use = [0] * 32
            for fc in range(32):
                half = fc % 2
                if fc >= 2:
                    p.wait("gpsimd", "tensor", t_w2use[fc - 2])
                tld = dma("gpsimd", "u%d" % half, a_w2c[:, half * 1024:half * 1024 + 1024],
                          bass.AP(l0_W2, fc * 128 * 1024, [[1024, 128], [1, 1024]]))
                p.waitc("tensor", "u%d" % half, tld)
                for dc in range(NS):
                    mm(a_w2c[:, half * 1024 + dc * 128:half * 1024 + dc * 128 + 128],
                       a_h1[:, fc * G1:(fc + 1) * G1], pb[dc][:, :G1],
                       start=(fc == 0), stop=(fc == 31),
                       inc=(1 if dc == NS - 1 else 0))
                t_w2use[fc] = p.tok("tensor")
            for dc in range(NS):
                bank_done(dc)
            t_down = p.tok("tensor")
            p.wait("vector", "tensor", t_down)
            for dc in range(NS):
                ve(lambda e, dc=dc, g=g: e.scalar_tensor_tensor(
                    xap(dc, g, G1), pb[dc][:, :G1], b2sb[:, dc:dc + 1],
                    xap(dc, g, G1), AL.add, AL.add))
                bank_rdone(dc, "vector")
            p.barrier()

    if phases >= 2:
        G2 = 256
        b_w1 = sb("b_w1", (128, 16384), BF16, S0)
        b_w2 = sb("b_w2", (128, 16384), BF16, S0 + 32768)
        b_wo = sb("b_wo", (128, 8192), BF16, S0 + 65536)
        b_yg = sb("b_yg", (128, 8 * G2), F32, S0 + 81920)
        b_ybf = sb("b_ybf", (128, 8 * G2), BF16, S0 + 90112)
        b_h1 = sb("b_h1", (128, 16 * G2), BF16, S0 + 94208)
        b_rsn = sb("b_rsn", (128, 8 * G2), BF16, S0 + 102400)
        b_sq2 = sb("b_sq2", (128, 512), BF16, S0 + 106496)
        b_rsc = sb("b_rsc", (128, 256), F32, S0 + 108544)
        b_tmpln = sb("b_tmpln", (128, 256), F32, S0 + 109568)
        assert S0 + 110592 <= SB_END

        xs = x[:, :].rearrange("p (s t) -> p s t", t=T)
        xl = xs[:, :, 2047:2048].rearrange("p s o -> p (s o)")
        tvl = ve(lambda e: e.tensor_copy(gsc[:, 0:8], xl))
        p.wait("tensor", "vector", tvl)
        mm_group(0, [(ones_f[:, :], gsc[:, 0:8])], pb[0][:, 0:8])
        bank_read(0, "vector")
        ve(lambda e: e.reduce_sum(gsc[:, 24:25], pb[0][:, 0:8], axis=AX))
        bank_rdone(0, "vector")
        ve(lambda e: e.tensor_scalar_mul(gsc[:, 24:25], gsc[:, 24:25], 1.0 / 1024.0))
        tvc = ve(lambda e: e.tensor_single_scalar(gsc[:, 8:16], gsc[:, 0:8],
                                                  gsc[:, 24:25], AL.subtract))
        p.wait("scalar", "vector", tvc)
        tsc = se(lambda e: e.activation(gsc[:, 16:24], gsc[:, 8:16], AF.Square))
        p.wait("tensor", "scalar", tsc)
        mm_group(1, [(ones_f[:, :], gsc[:, 16:24])], pb[1][:, 0:8])
        bank_read(1, "vector")
        ve(lambda e: e.reduce_sum(gsc[:, 25:26], pb[1][:, 0:8], axis=AX))
        bank_rdone(1, "vector")
        tvv = p.tok("vector")
        p.wait("scalar", "vector", tvv)
        tss = se(lambda e: e.activation(gsc[:, 26:27], gsc[:, 25:26], AF.Sqrt,
                                        bias=float(LN_EPS), scale=1.0 / 1024.0))
        p.wait("vector", "scalar", tss)
        ve(lambda e: e.reciprocal(gsc[:, 26:27], gsc[:, 26:27]))
        tyl = ve(lambda e: e.tensor_single_scalar(gsc[:, 32:40], gsc[:, 8:16],
                                                  gsc[:, 26:27], AL.mult))
        p.wait("tensor", "vector", tyl)
        mm_group(2, [(wp1sb[:, dc * 128:(dc + 1) * 128], gsc[:, 32 + dc:33 + dc])
                     for dc in range(NS)], pb[2][:, 0:1])
        bank_read(2, "scalar")
        thp = se(lambda e: e.activation(gsc[:, 27:28], pb[2][:, 0:1], AF.Gelu,
                                        bias=bp1sb[:, 0:1]))
        bank_rdone(2, "scalar")
        p.wait("tensor", "scalar", thp)
        mm_group(3, [(gsc[:, 27:28], wp2sb[:, :])], pb[3][0:1, 0:8])
        bank_read(3, "vector")
        ve(lambda e: e.tensor_add(gt[0:1, 0:8], pb[3][0:1, 0:8], bp2sb[0:1, :]))
        bank_rdone(3, "vector")
        ve(lambda e: e.reduce_max(gt[0:1, 8:9], gt[0:1, 0:8], axis=AX))
        ve(lambda e: e.tensor_single_scalar(gt[0:1, 16:24], gt[0:1, 0:8],
                                            gt[0:1, 8:9], AL.is_ge))
        ve(lambda e: e.scalar_tensor_tensor(gt[0:1, 16:24], gt[0:1, 16:24],
                                            -1048576.0, idxbig[0:1, :],
                                            AL.mult, AL.add))
        ve(lambda e: e.tensor_reduce(gt[0:1, 9:10], gt[0:1, 16:24], AX, AL.min))
        t_sel = ve(lambda e: e.tensor_copy(seli[0:1, 0:1], gt[0:1, 9:10]))

        p.wait("gpsimd", "vector", t_sel)
        p.ccnt["lg"] += 16 * 4

        def dyn_loads(e):
            sem = p.sems["lg"]
            with e.register("selr") as r, e.register("offr") as r2:
                e.reg_load(r, seli[0:1, 0:1])
                e.reg_mul(r2, r, 1024 * 2048)
                e.dma_start(out=b_w1[:, :],
                            in_=bass.AP(f0_W1, r2,
                                        [[2048, 128], [128 * 2048, 8], [1, 2048]])
                            ).then_inc(sem, 16)
                e.reg_mul(r2, r, 2048 * 1024)
                e.dma_start(out=b_w2[:, :],
                            in_=bass.AP(f0_W2, r2,
                                        [[1024, 128], [128 * 1024, 16], [1, 1024]])
                            ).then_inc(sem, 16)
                e.reg_mul(r2, r, 128 * 16)
                e.dma_start(out=f0b1s[:, :],
                            in_=bass.AP(f0_b1, r2, [[16, 128], [1, 16]])
                            ).then_inc(sem, 16)
                e.reg_mul(r2, r, 128 * 8)
                e.dma_start(out=f0b2s[:, :],
                            in_=bass.AP(f0_b2, r2, [[8, 128], [1, 8]])
                            ).then_inc(sem, 16)
            return e.nop()
        p.emit("gpsimd", dyn_loads, inc=0)
        dma("gpsimd", "lg", b_wo[:, :],
            bass.AP(f0_Wo, 0, [[1024, 128], [128 * 1024, 8], [1, 1024]]))
        t_wsel = p.ccnt["lg"]
        p.waitc("tensor", "lg", t_wsel)
        p.waitc("scalar", "lg", t_wsel)

        for g in range(8):
            t_lng = layernorm_cg(
                G2, lambda s, g=g: xap(s, g, G2),
                lambda s: b_yg[:, s * G2:(s + 1) * G2],
                [(lambda s: b_ybf[:, s * G2:(s + 1) * G2], None, None),
                 (lambda s: b_yg[:, s * G2:(s + 1) * G2], None, None)],
                b_sq2, b_rsc, b_tmpln, src_tok=0)
            p.wait("tensor", "vector", t_lng)
            for fc in range(16):
                bk = 2 + (fc % 2)
                mm_group(bk, [(b_w1[:, dc * 2048 + fc * 128:dc * 2048 + fc * 128 + 128],
                               b_ybf[:, dc * G2:(dc + 1) * G2]) for dc in range(NS)],
                         pb[bk][:, :G2])
                bank_read(bk, "scalar")
                se(lambda e, fc=fc, bk=bk: e.activation(
                    b_h1[:, fc * G2:(fc + 1) * G2], pb[bk][:, :G2], AF.Gelu,
                    bias=f0b1s[:, fc:fc + 1]))
                bank_rdone(bk, "scalar")
            t_h1p2 = p.tok("scalar")
            p.wait("tensor", "scalar", t_h1p2)
            for dc in range(NS):
                bk = 4 + (dc % 2)
                mm_group(bk, [(b_w2[:, fc * 1024 + dc * 128:fc * 1024 + dc * 128 + 128],
                               b_h1[:, fc * G2:(fc + 1) * G2]) for fc in range(16)],
                         pb[bk][:, :G2])
                bank_read(bk, "scalar")
                se(lambda e, dc=dc, bk=bk: e.activation(
                    b_rsn[:, dc * G2:(dc + 1) * G2], pb[bk][:, :G2], AF.Identity,
                    bias=f0b2s[:, dc:dc + 1]))
                bank_rdone(bk, "scalar")
            t_rsn = p.tok("scalar")
            p.wait("tensor", "scalar", t_rsn)
            for dc in range(NS):
                bk = 6 + (dc % 2)
                mm_group(bk, [(b_wo[:, kc * 1024 + dc * 128:kc * 1024 + dc * 128 + 128],
                               b_rsn[:, kc * G2:(kc + 1) * G2]) for kc in range(NS)],
                         pb[bk][:, :G2])
                bank_read(bk, "vector")
                ve(lambda e, dc=dc, bk=bk: e.scalar_tensor_tensor(
                    b_tmpln[:, :G2], pb[bk][:, :G2], 0.3,
                    b_yg[:, dc * G2:(dc + 1) * G2], AL.mult, AL.add))
                ve(lambda e, dc=dc, g=g: e.scalar_tensor_tensor(
                    xap(dc, g, G2), b_tmpln[:, :G2], f0bo3[:, dc:dc + 1],
                    xap(dc, g, G2), AL.add, AL.add))
                bank_rdone(bk, "vector")
        p.barrier()

    if phases >= 3:
        G3 = 512
        c_wqkv = sb("c_wqkv", (128, 24576), BF16, S0)
        c_wo1 = sb("c_wo1", (128, 8192), BF16, S0 + 49152)
        c_invt = sb("c_invt", (128, T), F32, S0 + 65536)
        c_ybf = sb("c_ybf", (128, 4096), BF16, S0 + 73728)
        c_qdm = sb("c_qdm", (128, 4096), BF16, S0 + 81920)
        c_kdm = sb("c_kdm", (128, 4096), BF16, S0 + 90112)
        c_xc = sb("c_xc", (128, 4096), F32, S0 + 81920)
        c_ktm = sb("c_ktm", (128, 4096), BF16, S0 + 98304)
        c_vtm = sb("c_vtm", (128, 4096), BF16, S0 + 106496)
        c_heads = sb("c_heads", (128, 4096), BF16, S0 + 73728)
        c_pref = sb("c_pref", (128, 512), F32, S0 + 114688)
        c_prbf = sb("c_prbf", (128, 64), BF16, S0 + 116736)
        c_sTb = sb("c_sTb", (128, 128), BF16, S0 + 116864)
        c_sq2 = sb("c_sq2", (128, 1024), BF16, S0 + 117120)
        c_rsc = sb("c_rsc", (128, 512), F32, S0 + 121088)
        c_tmpln = c_rsc
        assert S0 + 123136 <= SB_END

        for dc in range(NS):
            dma("gpsimd", "lg", c_wqkv[:, dc * 3072:(dc + 1) * 3072],
                bass.AP(l1_wqkv, dc * 128 * 3072, [[3072, 128], [1, 3072]]))
        dma("gpsimd", "lg", c_wo1[:, :],
            bass.AP(l1_wo, 0, [[1024, 128], [128 * 1024, 8], [1, 1024]]))
        t_w3i = dma("sync", "ld", c_invt[:, :], c_invtot.ap())
        t_w3 = p.ccnt["lg"]
        p.waitc("tensor", "lg", t_w3)
        p.waitc("vector", "lg", t_w3)
        p.waitc("vector", "ld", t_w3i)
        tvz = ve(lambda e: e.memset(c_pref[:, :], 0.0))

        t_pref_use = [0, 0]
        t_sTb_use = [0, 0]
        for g in range(4):
            t_ln3 = layernorm_cg(
                G3, lambda s, g=g: xap(s, g, G3),
                lambda s: c_xc[:, s * G3:(s + 1) * G3],
                [(lambda s: c_ybf[:, s * G3:(s + 1) * G3], None, None)],
                c_sq2, c_rsc, c_tmpln, src_tok=0)
            p.wait("tensor", "vector", t_ln3)
            for o_ in range(NS):
                bk = o_ % 2
                mm_group(bk, [(c_wqkv[:, dc * 3072 + o_ * 128:dc * 3072 + o_ * 128 + 128],
                               c_ybf[:, dc * G3:(dc + 1) * G3]) for dc in range(NS)],
                         pb[bk][:, :G3])
                bank_read(bk, "scalar")
                se(lambda e, o_=o_, bk=bk: e.activation(
                    c_qdm[:, o_ * G3:(o_ + 1) * G3], pb[bk][:, :G3], AF.Identity))
                bank_rdone(bk, "scalar")
            for o_ in range(NS):
                bk = o_ % 2
                mm_group(bk, [(c_wqkv[:, dc * 3072 + 1024 + o_ * 128:
                                      dc * 3072 + 1024 + o_ * 128 + 128],
                               c_ybf[:, dc * G3:(dc + 1) * G3]) for dc in range(NS)],
                         pb[bk][:, :G3])
                bank_read(bk, "vector")
                ve(lambda e, o_=o_, bk=bk: e.tensor_scalar(
                    c_kdm[:, o_ * G3:(o_ + 1) * G3], pb[bk][:, :G3], 0.0, 0.5,
                    AL.is_ge, AL.subtract))
                bank_rdone(bk, "vector")
            for (dst, col0) in ((c_ktm, 1024), (c_vtm, 2048)):
                for tt in range(4):
                    for hf in range(2):
                        bk = (tt * 2 + hf) % 2
                        mm_group(bk, [(c_ybf[:, dc * G3 + tt * 128:dc * G3 + tt * 128 + 128],
                                       c_wqkv[:, dc * 3072 + col0 + hf * 512:
                                              dc * 3072 + col0 + hf * 512 + 512])
                                      for dc in range(NS)], pb[bk][:, :512])
                        bank_read(bk, "vector")
                        ve(lambda e, dst=dst, tt=tt, hf=hf, bk=bk: e.tensor_scalar(
                            dst[:, tt * 1024 + hf * 512:tt * 1024 + hf * 512 + 512],
                            pb[bk][:, :512], 0.0, 0.5, AL.is_ge, AL.subtract))
                        bank_rdone(bk, "vector")
            t_qkv = p.tok("vector")
            p.wait("tensor", "vector", t_qkv)
            p.wait("vector", "vector", 0)
            if g == 0:
                p.wait("tensor", "vector", tvz)

            for nl in range(8):
                for h in range(16):
                    par = h % 2
                    pof = par * 64
                    so = (h // 2) * G3 + nl * 64
                    to = (nl // 2) * 1024 + h * 64
                    tp = (nl % 2) * 64
                    p.wait("vector", "tensor", t_pref_use[par])
                    tpb = ve(lambda e, pof=pof, h=h: e.tensor_copy(
                        c_prbf[pof:pof + 64, :],
                        c_pref[pof:pof + 64, (h // 2) * 64:(h // 2) * 64 + 64]))
                    bk = 2 + par
                    mm_group(bk, [(c_kdm[pof:pof + 64, so:so + 64],
                                   c_qdm[pof:pof + 64, so:so + 64])],
                             pb[bk][tp:tp + 64, 0:64])
                    bank_read(bk, "vector")
                    p.wait("vector", "tensor", t_sTb_use[par])
                    tsb = ve(lambda e, tp=tp, par=par, bk=bk: e.tensor_mul(
                        c_sTb[tp:tp + 64, par * 64:par * 64 + 64],
                        pb[bk][tp:tp + 64, 0:64], maskT[tp:tp + 64, :]))
                    bank_rdone(bk, "vector")
                    bk2 = 4 + par
                    bank_prep(bk2)
                    p.wait("tensor", "vector", tpb)
                    mm(c_prbf[pof:pof + 64, :], c_qdm[pof:pof + 64, so:so + 64],
                       pb[bk2][pof:pof + 64, 0:64], start=True, stop=False)
                    p.wait("tensor", "vector", tsb)
                    mm(c_vtm[tp:tp + 64, to:to + 64],
                       c_sTb[tp:tp + 64, par * 64:par * 64 + 64],
                       pb[bk2][pof:pof + 64, 0:64], start=False, stop=True, inc=1)
                    t_pref_use[par] = p.tok("tensor")
                    t_sTb_use[par] = p.tok("tensor")
                    bank_done(bk2)
                    bank_read(bk2, "vector")
                    ve(lambda e, pof=pof, so=so, g=g, nl=nl, bk2=bk2: e.tensor_mul(
                        c_heads[pof:pof + 64, so:so + 64],
                        pb[bk2][pof:pof + 64, 0:64],
                        c_invt[pof:pof + 64, g * G3 + nl * 64:g * G3 + nl * 64 + 64]))
                    bank_rdone(bk2, "vector")
                    bk3 = 6 + par
                    mm_group(bk3, [(c_ktm[tp:tp + 64, to:to + 64],
                                    c_vtm[tp:tp + 64, to:to + 64])],
                             pb[bk3][pof:pof + 64, 0:64])
                    bank_read(bk3, "vector")
                    ve(lambda e, pof=pof, h=h, bk3=bk3: e.tensor_add(
                        c_pref[pof:pof + 64, (h // 2) * 64:(h // 2) * 64 + 64],
                        c_pref[pof:pof + 64, (h // 2) * 64:(h // 2) * 64 + 64],
                        pb[bk3][pof:pof + 64, 0:64]))
                    bank_rdone(bk3, "vector")
            t_hd3 = p.tok("vector")
            p.wait("tensor", "vector", t_hd3)
            for dc in range(NS):
                bk = dc % 2
                mm_group(bk, [(c_wo1[:, kc * 1024 + dc * 128:kc * 1024 + dc * 128 + 128],
                               c_heads[:, kc * G3:(kc + 1) * G3]) for kc in range(NS)],
                         pb[bk][:, :G3])
                bank_read(bk, "vector")
                ve(lambda e, dc=dc, g=g, bk=bk: e.tensor_add(
                    xap(dc, g, G3), xap(dc, g, G3), pb[bk][:, :G3]))
                bank_rdone(bk, "vector")
        p.barrier()

    if phases >= 4:
        d_ybf = sb("d_ybf", (128, 16384), BF16, S0)
        d_h2 = sb("d_h2", (128, 32768), BF16, S0 + 32768)
        d_xc = sb("d_xc", (128, 16384), F32, S0 + 32768)
        d_ye = sb("d_ye", (128, 8192), BF16, S0 + 98304)
        d_wd = sb("d_wd", (128, 4096), BF16, S0 + 114688)
        d_wg = sb("d_wg", (128, 2048), BF16, S0 + 114688)
        d_wu = sb("d_wu", (128, 2048), BF16, S0 + 118784)
        d_sq2 = sb("d_sq2", (128, 1024), BF16, S0 + 98304)
        d_rsc = sb("d_rsc", (128, 512), F32, S0 + 102400)
        d_tmpln = d_rsc
        assert S0 + 122880 <= SB_END

        for cg in range(4):
            layernorm_cg(
                512, lambda s, cg=cg: x[:, s * T + cg * 512:s * T + cg * 512 + 512],
                lambda s, cg=cg: d_xc[:, s * 2048 + cg * 512:s * 2048 + cg * 512 + 512],
                [(lambda s, cg=cg: d_ybf[:, s * 2048 + cg * 512:s * 2048 + cg * 512 + 512],
                  None, None)],
                d_sq2, d_rsc, d_tmpln, src_tok=0)
        t_ln4 = p.tok("vector")
        p.wait("tensor", "vector", t_ln4)

        for tt in range(16):
            bk = tt % 2
            mm_group(bk, [(d_ybf[:, dc * 2048 + tt * 128:dc * 2048 + tt * 128 + 128],
                           gatebf[:, dc * 8:(dc + 1) * 8]) for dc in range(NS)],
                     pb[bk][:, 0:8])
            bank_read(bk, "vector")
            ve(lambda e, bk=bk: e.tensor_copy(gsc[:, 0:8], pb[bk][:, 0:8]))
            bank_rdone(bk, "vector")
            ve(lambda e: e.reduce_max(gsc[:, 32:33], gsc[:, 0:8], axis=AX))
            ve(lambda e: e.tensor_single_scalar(gsc[:, 8:16], gsc[:, 0:8],
                                                gsc[:, 32:33], AL.is_ge))
            ve(lambda e: e.scalar_tensor_tensor(gsc[:, 16:24], gsc[:, 8:16], -1e30,
                                                gsc[:, 0:8], AL.mult, AL.add))
            ve(lambda e: e.reduce_max(gsc[:, 33:34], gsc[:, 16:24], axis=AX))
            ve(lambda e: e.tensor_single_scalar(gsc[:, 24:32], gsc[:, 16:24],
                                                gsc[:, 33:34], AL.is_ge))
            tnv = ve(lambda e: e.tensor_scalar_mul(gsc[:, 34:35], gsc[:, 32:33], -1.0))
            p.wait("scalar", "vector", tnv)
            tp2 = se(lambda e: e.activation(gsc[:, 35:36], gsc[:, 33:34], AF.Sigmoid,
                                            bias=gsc[:, 34:35]))
            p.wait("vector", "scalar", tp2)
            ve(lambda e: e.tensor_scalar(gsc[:, 36:37], gsc[:, 35:36], -1.0, 1.0,
                                         AL.mult, AL.add))
            ve(lambda e: e.tensor_single_scalar(gsc[:, 40:48], gsc[:, 8:16],
                                                gsc[:, 36:37], AL.mult))
            tcb = ve(lambda e: e.scalar_tensor_tensor(gsc[:, 40:48], gsc[:, 24:32],
                                                      gsc[:, 35:36], gsc[:, 40:48],
                                                      AL.mult, AL.add))
            p.wait("tensor", "vector", tcb)
            bk2 = 2 + tt % 2
            bank_prep(bk2)
            p.emit("tensor", lambda e, bk2=bk2: e.transpose(
                pb[bk2][0:8, 0:128], gsc[:, 40:48], ident[:, :]), inc=1)
            bank_done(bk2)
            bank_read(bk2, "vector")
            ve(lambda e, tt=tt, bk2=bk2: e.tensor_copy(
                comb[:, tt * 128:(tt + 1) * 128], pb[bk2][0:8, 0:128]))
            bank_rdone(bk2, "vector")
        t_comb = p.tok("vector")
        p.wait("tensor", "vector", t_comb)

        iters = [(G, e_) for G in range(2) for e_ in range(8)]

        def build_ye(G, e_):
            for hf in range(2):
                bk = 6 + hf
                mm_group(bk, [(bcast8[:, e_ * 128:(e_ + 1) * 128],
                               comb[:, G * 1024 + hf * 512:G * 1024 + hf * 512 + 512])],
                         pb[bk][:, :512])
                bank_read(bk, "vector")
                for s in range(NS):
                    ve(lambda e, s=s, hf=hf, G=G, bk=bk: e.tensor_mul(
                        d_ye[:, s * 1024 + hf * 512:s * 1024 + hf * 512 + 512],
                        d_ybf[:, s * 2048 + G * 1024 + hf * 512:
                              s * 2048 + G * 1024 + hf * 512 + 512],
                        pb[bk][:, :512]))
                bank_rdone(bk, "vector")
            return p.tok("vector")

        def issue_wgwu(G, e_, fc, par):
            dma("gpsimd", "w%d" % par, d_wg[:, par * 1024:par * 1024 + 1024],
                bass.AP(f1_wg, e_ * D * 4096 + fc * 128,
                        [[4096, 128], [128 * 4096, 8], [1, 128]]))
            return dma("gpsimd", "w%d" % par,
                       d_wu[:, par * 1024:par * 1024 + 1024],
                       bass.AP(f1_wu, e_ * D * 4096 + fc * 128,
                               [[4096, 128], [128 * 4096, 8], [1, 128]]))

        t_ye = build_ye(*iters[0])
        t_wguse = [0] * 32
        t_wd_last = 0
        for it, (G, e_) in enumerate(iters):
            if it > 0:
                p.wait("gpsimd", "tensor", t_wd_last)
            t_wld = [issue_wgwu(G, e_, 0, 0), issue_wgwu(G, e_, 1, 1)]
            p.wait("tensor", "vector", t_ye)
            for fc in range(32):
                par = fc % 2
                p.waitc("tensor", "w%d" % par, t_wld[par])
                for hf in range(2):
                    bkg = hf
                    mm_group(bkg, [(d_wg[:, par * 1024 + dc * 128:par * 1024 + dc * 128 + 128],
                                    d_ybf[:, dc * 2048 + G * 1024 + hf * 512:
                                          dc * 2048 + G * 1024 + hf * 512 + 512])
                                   for dc in range(NS)], pb[bkg][:, :512])
                    bku = 2 + hf
                    mm_group(bku, [(d_wu[:, par * 1024 + dc * 128:par * 1024 + dc * 128 + 128],
                                    d_ye[:, dc * 1024 + hf * 512:
                                         dc * 1024 + hf * 512 + 512])
                                   for dc in range(NS)], pb[bku][:, :512])
                t_wguse[fc] = p.tok("tensor")
                nxt = fc + 2
                if nxt < 32:
                    p.wait("gpsimd", "tensor", t_wguse[fc])
                    t_wld[par] = issue_wgwu(G, e_, nxt, par)
                for hf in range(2):
                    bkg, bku = hf, 2 + hf
                    bank_read(bkg, "scalar")
                    tsl = se(lambda e, fc=fc, hf=hf, bkg=bkg: e.activation(
                        d_h2[:, fc * 1024 + hf * 512:fc * 1024 + hf * 512 + 512],
                        pb[bkg][:, :512], AF.Silu))
                    bank_rdone(bkg, "scalar")
                    p.wait("vector", "scalar", tsl)
                    bank_read(bku, "vector")
                    ve(lambda e, fc=fc, hf=hf, bku=bku: e.tensor_mul(
                        d_h2[:, fc * 1024 + hf * 512:fc * 1024 + hf * 512 + 512],
                        d_h2[:, fc * 1024 + hf * 512:fc * 1024 + hf * 512 + 512],
                        pb[bku][:, :512]))
                    bank_rdone(bku, "vector")
            t_h2 = p.tok("vector")
            p.wait("tensor", "vector", t_h2)
            p.wait("gpsimd", "tensor", t_wguse[31])
            t_wduse = {}
            for dc in range(NS):
                for half in range(2):
                    par = (dc * 2 + half) % 2
                    if dc * 2 + half >= 2:
                        p.wait("gpsimd", "tensor", t_wduse[dc * 2 + half - 2])
                    tld = dma("gpsimd", "u%d" % par,
                              d_wd[:, par * 2048:par * 2048 + 2048],
                              bass.AP(f1_wd,
                                      e_ * 4096 * D + half * 16 * 128 * 1024 + dc * 128,
                                      [[1024, 128], [128 * 1024, 16], [1, 128]]))
                    p.waitc("tensor", "u%d" % par, tld)
                    for hf in range(2):
                        bk = 4 + hf
                        bank_prep(bk) if (half == 0) else None
                        for j, fc in enumerate(range(half * 16, half * 16 + 16)):
                            mm(d_wd[:, par * 2048 + j * 128:par * 2048 + j * 128 + 128],
                               d_h2[:, fc * 1024 + hf * 512:fc * 1024 + hf * 512 + 512],
                               pb[bk][:, :512], start=(fc == 0), stop=(fc == 31),
                               inc=(1 if j == 15 else 0))
                        if half == 1:
                            bank_done(bk)
                    t_wduse[dc * 2 + half] = p.tok("tensor")
                    t_wd_last = p.tok("tensor")
                for hf in range(2):
                    bk = 4 + hf
                    bank_read(bk, "vector")
                    ve(lambda e, dc=dc, hf=hf, G=G, bk=bk: e.tensor_add(
                        x[:, dc * T + G * 1024 + hf * 512:
                          dc * T + G * 1024 + hf * 512 + 512],
                        x[:, dc * T + G * 1024 + hf * 512:
                          dc * T + G * 1024 + hf * 512 + 512],
                        pb[bk][:, :512]))
                    bank_rdone(bk, "vector")
                if dc == 0 and it + 1 < len(iters):
                    t_ye = build_ye(*iters[it + 1])
        p.barrier()

    p.barrier(engs=("sync",))
    tout = dma("sync", "out", bass.AP(out_d, 0, [[T, 128], [128 * T, NS], [1, T]]),
               x[:, :])
    p.emit("sync", lambda e, t=tout: e.wait_ge(p.sems["out"], t))

    p.finalize()
    return nc


LN_EPS_H = 1e-5
PI_2_H = math.pi / 2


def prep_inputs(inputs):
    f = np.float32
    c = np.ascontiguousarray
    WQ = inputs["L0_attn_W_Q"]; WK = inputs["L0_attn_W_K"]
    WV = inputs["L0_attn_W_V"]; WO = inputs["L0_attn_W_O"]
    base = {
        "l0_wq16": c(WQ[:, :, 0].T.astype(f)),
        "l0_wk16": c(WK[:, :, 0].T.astype(f)),
        "l0_wv": c(WV.transpose(1, 0, 2).reshape(1024, 1024).astype(f)),
        "l0_wo": c(WO.transpose(2, 1, 0).reshape(1024, 1024).astype(f)),
        "l0_ln1w": c(inputs["L0_attn_ln1_w"].reshape(8, 128).T.astype(f)),
        "l0_ln1b": c(inputs["L0_attn_ln1_b"].reshape(8, 128).T.astype(f)),
        "l0_ln2w": c(inputs["L0_attn_ln2_w"].reshape(8, 128).T.astype(f)),
        "l0_ln2b": c(inputs["L0_attn_ln2_b"].reshape(8, 128).T.astype(f)),
        "l0_W1": c(inputs["L0_attn_W1"].astype(f)),
        "l0_b1": c(inputs["L0_attn_b1"].reshape(32, 128).T.astype(f)),
        "l0_W2": c(inputs["L0_attn_W2"].astype(f)),
        "l0_b2": c(inputs["L0_attn_b2"].reshape(8, 128).T.astype(f)),
        "f0_Wp1": c(inputs["L0_ffn_Wp1"].astype(f)),
        "f0_bp1": c(inputs["L0_ffn_bp1"].reshape(128, 1).astype(f)),
        "f0_Wp2": c(inputs["L0_ffn_Wp2"].astype(f)),
        "f0_bp2": c(inputs["L0_ffn_bp2"].reshape(1, 8).astype(f)),
        "f0_W1": c(inputs["L0_ffn_W1"].astype(f)),
        "f0_b1": c(inputs["L0_ffn_b1"].reshape(8, 16, 128).transpose(0, 2, 1).astype(f)),
        "f0_W2": c(inputs["L0_ffn_W2"].astype(f)),
        "f0_b2": c(inputs["L0_ffn_b2"].reshape(8, 8, 128).transpose(0, 2, 1).astype(f)),
        "f0_Wo": c(inputs["L0_ffn_Wo"].astype(f)),
        "f0_bo": c(inputs["L0_ffn_bo"].reshape(8, 128).T.astype(f)),
        "l1_wqkv": c(inputs["L1_attn_W_qkv"].T.astype(f)),
        "l1_wo": c(inputs["L1_attn_W_o"].T.astype(f)),
        "f1_gate": c(inputs["L1_ffn_gate_w"].T.astype(f)),
        "f1_wg": c(inputs["L1_ffn_wg"].transpose(0, 2, 1).astype(f)),
        "f1_wu": c(inputs["L1_ffn_wu"].transpose(0, 2, 1).astype(f)),
        "f1_wd": c(inputs["L1_ffn_wd"].transpose(0, 2, 1).astype(f)),
        "c_ones_bf": np.ones((128, 128), dtype=ml_dtypes.bfloat16),
        "c_ones_f": np.ones((128, 128), dtype=f),
        "c_ident": np.eye(128, dtype=f),
        "c_idxbig": (np.arange(8, dtype=f) + 1048576.0).reshape(1, 8),
        "c_flt": c(np.broadcast_to(
            np.array([LN_EPS_H, PI_2_H], dtype=f), (128, 2)).astype(f)),
    }
    sel = np.zeros((16, 1024), dtype=f)
    for k in range(16):
        s, r = divmod(k, 2)
        sel[k, s * 128 + r * 64:s * 128 + r * 64 + 64] = 1.0
    base["c_sel16"] = sel
    bc8 = np.zeros((8, 1024), dtype=f)
    for e in range(8):
        bc8[e, e * 128:(e + 1) * 128] = 1.0
    base["c_bcast8"] = bc8
    mT = np.triu(np.ones((64, 64), dtype=f))
    base["c_maskT"] = c(np.concatenate([mT, mT], axis=0))
    base["c_invtot"] = c(np.broadcast_to(
        0.5 / (np.arange(T, dtype=np.float64) + 1.0), (128, T)).astype(f))
    xs = inputs["L0_attn_x"]
    maps = []
    for b in range(8):
        m = dict(base)
        m["x_in"] = c(xs[b].T.astype(f))
        maps.append(m)
    return maps


def kernel(**inputs):
    maps = prep_inputs(inputs)
    nc = build_core_program(4)
    res = run_bass_kernel_spmd(nc, maps, core_ids=list(range(8)))
    out = np.stack([np.asarray(r["out"]).T for r in res.results], axis=0)
    return out.astype(np.float32)
